# version 21
# baseline (speedup 1.0000x reference)
import numpy as np
import ml_dtypes
from contextlib import ExitStack

import concourse.bass as bass
import concourse.mybir as mybir
from concourse.bass_utils import run_bass_kernel_spmd

F32 = mybir.dt.float32
BF16 = mybir.dt.bfloat16
U8 = mybir.dt.uint8
I32 = mybir.dt.int32
AF = mybir.ActivationFunctionType
ALU = mybir.AluOpType
AX = mybir.AxisListType

DEBUG = 0

B = 2
S = 4096
D = 2048
DK = 128
NL = 4
NC = 8
ROWS = 1024
NT = 16
STILE = 512
QT = 8
HBS = 4
VST = 129
VHB = 32 * VST
GRP = 2
GR = ROWS // GRP
IH = 2
NSLAB = 16
INV_SQRT_DK = float(1.0 / np.sqrt(128.0))

_PERM = np.concatenate([np.arange(0, 128, 2), np.arange(1, 128, 2)])


def _stage(i):
    bf = ml_dtypes.bfloat16
    x = np.ascontiguousarray(np.asarray(i["L0_attn_x"], np.float32).reshape(B * S, D))

    Wq = np.asarray(i["L0_attn_Wq"], np.float32)
    Wk = np.asarray(i["L0_attn_Wk"], np.float32)
    Wv = np.asarray(i["L0_attn_Wv"], np.float32)
    Wo = np.asarray(i["L0_attn_Wo"], np.float32)
    Aq = np.asarray(i["L0_attn_Aq"], np.float32)
    Ak = np.asarray(i["L0_attn_Ak"], np.float32)
    Av = np.asarray(i["L0_attn_Av"], np.float32)
    Ao = np.asarray(i["L0_attn_Ao"], np.float32)
    Bq = np.asarray(i["L0_attn_Bq"], np.float32)
    Bk = np.asarray(i["L0_attn_Bk"], np.float32)
    Bv = np.asarray(i["L0_attn_Bv"], np.float32)
    Bo = np.asarray(i["L0_attn_Bo"], np.float32)

    waT = np.zeros((D, NL * 96), np.float32)
    for a in range(NL):
        waT[:, a * 96 + 0: a * 96 + 16] = Aq[a].T
        waT[:, a * 96 + 32: a * 96 + 48] = Ak[a].T
        waT[:, a * 96 + 64: a * 96 + 80] = Av[a].T
    waoT = np.zeros((D, NL * 16), np.float32)
    for a in range(NL):
        waoT[:, a * 16:(a + 1) * 16] = Ao[a].T
    wboT = np.zeros((32, NL * D), np.float32)
    for a in range(NL):
        wboT[0:16, a * D:(a + 1) * D] = Bo[a].T

    woT_t = np.ascontiguousarray(
        Wo.reshape(16, 128, 16, 128).transpose(0, 3, 2, 1)).reshape(16, 128, D)

    gw = np.asarray(i["L0_ffn_gate_w"])
    uw = np.asarray(i["L0_ffn_up_w"])
    dw = np.asarray(i["L0_ffn_down_w"])
    g_u8 = np.ascontiguousarray(
        gw.reshape(32, 256, 16, 128).transpose(0, 3, 2, 1)).astype(bf).reshape(32, 128, 4096)
    u_u8 = np.ascontiguousarray(
        uw.reshape(32, 256, 16, 128).transpose(0, 3, 2, 1)).astype(bf).reshape(32, 128, 4096)
    d_u8 = np.ascontiguousarray(
        dw.reshape(16, 128, 64, 128).transpose(0, 3, 2, 1)).astype(bf).reshape(16, 128, 8192)

    gsc = np.ascontiguousarray(np.asarray(i["L0_ffn_gate_scale"], np.float32).reshape(64, 128).T)
    usc = np.ascontiguousarray(np.asarray(i["L0_ffn_up_scale"], np.float32).reshape(64, 128).T)
    dsc = np.ascontiguousarray(np.asarray(i["L0_ffn_down_scale"], np.float32).reshape(16, 128).T)

    starts = np.asarray(i["L0_attn_lora_start_indices"], np.int32).reshape(1, NL)
    iota4 = np.arange(NL, dtype=np.float32).reshape(1, NL)

    inv_freq = 10000.0 ** (-np.arange(0, DK, 2, dtype=np.float64) / DK)
    ang = inv_freq[:, None] * np.arange(S, dtype=np.float64)[None, :]
    rope_cos = np.concatenate([np.cos(ang), np.cos(ang)], 0).astype(np.float32)
    rope_sin = np.concatenate([np.sin(ang), np.sin(ang)], 0).astype(np.float32)

    kk = np.arange(128)[:, None]
    qq = np.arange(512)[None, :]
    cmask = np.concatenate(
        [(qq >= kk + v * 128).astype(np.float32) for v in range(4)], axis=1
    ).astype(bf)
    ident = np.eye(128, dtype=np.float32)

    shared = {
        "waT": waT, "waoT": waoT, "wboT": wboT, "woT_t": woT_t,
        "g_u8": g_u8, "u_u8": u_u8, "d_u8": d_u8,
        "gsc": gsc, "usc": usc, "dsc": dsc,
        "starts": starts, "iota4": iota4,
        "rope_cos": rope_cos, "rope_sin": rope_sin,
        "cmask": np.ascontiguousarray(cmask), "ident": ident,
    }

    in_maps = []
    for c in range(NC):
        h0 = 2 * c
        cols = []
        for Wm, perm in ((Wq, True), (Wk, True), (Wv, False)):
            for hh in (h0, h0 + 1):
                rows = hh * DK + (_PERM if perm else np.arange(DK))
                cols.append(Wm[rows, :].T)
        wqkvT = np.ascontiguousarray(np.concatenate(cols, axis=1))
        wbT = np.zeros((96, NL * 256), np.float32)
        for a in range(NL):
            for pi, (Bm, perm) in enumerate(((Bq, True), (Bk, True), (Bv, False))):
                for hi, hh in enumerate((h0, h0 + 1)):
                    rows = hh * DK + (_PERM if perm else np.arange(DK))
                    wbT[pi * 32: pi * 32 + 16,
                        a * 256 + hi * 128: a * 256 + (hi + 1) * 128] = Bm[a][rows, :].T
        bsel = np.zeros((1, 2), np.float32)
        bsel[0, c // 4] = 1.0
        m = dict(shared)
        m.update({
            "x_rows": np.ascontiguousarray(x[c * ROWS:(c + 1) * ROWS]),
            "wqkvT": wqkvT,
            "wbT": np.ascontiguousarray(wbT),
            "bsel": bsel,
        })
        in_maps.append(m)
    return in_maps


class Sem:
    def __init__(self, ctx, nc, name):
        self.h = ctx.enter_context(nc.semaphore(name))
        self.n = 0

    def inc(self, instr, k=1):
        instr.then_inc(self.h, k)
        self.n += k
        return self.n


def _kcs(qt):
    return list(range(4 * qt + 4))


_ALL_IDX = []
for _hb in range(HBS):
    for _qt in range(QT):
        for _kc in _kcs(_qt):
            _ALL_IDX.append((_hb, _qt, _kc, _kc >= 4 * _qt))
IDX_OF = {(h, q, k): i for i, (h, q, k, d) in enumerate(_ALL_IDX)}
MASK_RANK = {}
for _i, _e in enumerate(_ALL_IDX):
    if _e[3]:
        MASK_RANK[_i] = len(MASK_RANK)
QT_END = {}
for _i, (_hb, _qt, _kc, _d) in enumerate(_ALL_IDX):
    QT_END[(_hb, _qt)] = _i
NIDX = len(_ALL_IDX)


def _build():
    nc = bass.Bass("TRN2", target_bir_lowering=False, debug=False)

    x_rows = nc.dram_tensor("x_rows", [ROWS, D], F32, kind="ExternalInput").ap()
    wqkvT = nc.dram_tensor("wqkvT", [D, 768], F32, kind="ExternalInput").ap()
    waT = nc.dram_tensor("waT", [D, NL * 96], F32, kind="ExternalInput").ap()
    wbT = nc.dram_tensor("wbT", [96, NL * 256], F32, kind="ExternalInput").ap()
    waoT = nc.dram_tensor("waoT", [D, NL * 16], F32, kind="ExternalInput").ap()
    wboT = nc.dram_tensor("wboT", [32, NL * D], F32, kind="ExternalInput").ap()
    woT_t = nc.dram_tensor("woT_t", [16, 128, D], F32, kind="ExternalInput").ap()
    g_u8 = nc.dram_tensor("g_u8", [IH * NSLAB, 128, 4096], BF16, kind="ExternalInput").ap()
    u_u8 = nc.dram_tensor("u_u8", [IH * NSLAB, 128, 4096], BF16, kind="ExternalInput").ap()
    d_u8 = nc.dram_tensor("d_u8", [16, 128, 8192], BF16, kind="ExternalInput").ap()
    gsc_d = nc.dram_tensor("gsc", [128, 64], F32, kind="ExternalInput").ap()
    usc_d = nc.dram_tensor("usc", [128, 64], F32, kind="ExternalInput").ap()
    dsc_d = nc.dram_tensor("dsc", [128, 16], F32, kind="ExternalInput").ap()
    starts_d = nc.dram_tensor("starts", [1, NL], I32, kind="ExternalInput").ap()
    iota4_d = nc.dram_tensor("iota4", [1, NL], F32, kind="ExternalInput").ap()
    cos_d = nc.dram_tensor("rope_cos", [128, S], F32, kind="ExternalInput").ap()
    sin_d = nc.dram_tensor("rope_sin", [128, S], F32, kind="ExternalInput").ap()
    cmask_d = nc.dram_tensor("cmask", [128, 2048], BF16, kind="ExternalInput").ap()
    ident_d = nc.dram_tensor("ident", [128, 128], F32, kind="ExternalInput").ap()
    bsel_d = nc.dram_tensor("bsel", [1, 2], F32, kind="ExternalInput").ap()

    y_out = nc.dram_tensor("y", [ROWS, D], F32, kind="ExternalOutput").ap()

    xn_mine = nc.dram_tensor("xn_mine", [ROWS, D], BF16, kind="Internal").ap()
    xn_allA = nc.dram_tensor("xn_allA", [NC * 512, D], BF16, kind="Internal",
                             addr_space="Shared").ap()
    xn_allB = nc.dram_tensor("xn_allB", [NC * 512, D], BF16, kind="Internal",
                             addr_space="Shared").ap()
    at_mine = nc.dram_tensor("at_mine", [B * S, 256], BF16, kind="Internal").ap()
    wo_bf = nc.dram_tensor("wo_bf", [16, 128, D], BF16, kind="Internal").ap()
    at_allA = nc.dram_tensor("at_allA", [NC * S, 256], BF16, kind="Internal",
                             addr_space="Shared").ap()
    at_allB = nc.dram_tensor("at_allB", [NC * S, 256], BF16, kind="Internal",
                             addr_space="Shared").ap()

    RG = [[0, 1, 2, 3, 4, 5, 6, 7]]

    dbg = {}
    if DEBUG >= 1:
        dbg["xn"] = nc.dram_tensor("dbg_xn", [NC * ROWS, D], BF16, kind="ExternalOutput").ap()
        dbg["q"] = nc.dram_tensor("dbg_q", [128, HBS * S], BF16, kind="ExternalOutput").ap()
        dbg["k"] = nc.dram_tensor("dbg_k", [128, HBS * S], BF16, kind="ExternalOutput").ap()
        dbg["v"] = nc.dram_tensor("dbg_v", [128, HBS * VHB], BF16, kind="ExternalOutput").ap()
        dbg["low"] = nc.dram_tensor("dbg_low", [96, 2 * STILE], BF16, kind="ExternalOutput").ap()
    if DEBUG >= 2:
        dbg["at"] = nc.dram_tensor("dbg_at", [B * S, 256], BF16, kind="ExternalOutput").ap()
        dbg["cbc"] = nc.dram_tensor("dbg_cbc", [128, 32], F32, kind="ExternalOutput").ap()
    if DEBUG >= 3:
        dbg["acc"] = nc.dram_tensor("dbg_acc", [128, 16 * GR], F32, kind="ExternalOutput").ap()
        dbg["h"] = nc.dram_tensor("dbg_h", [128, 32 * GR], BF16, kind="ExternalOutput").ap()

    ctx = ExitStack()
    with ctx:
        def sb(nm, shp, dt):
            return ctx.enter_context(nc.sbuf_tensor(nm, shp, dt))

        ones_col = sb("ones_col", [128, 1], BF16)
        ones_f32 = sb("ones_f32", [1, 128], F32)
        ones_rbf = sb("ones_rbf", [1, 128], BF16)
        small = sb("small", [128, 8], F32)
        mrow = sb("mrow", [1, 64], F32)
        cqt_bf = sb("cqt_bf", [1, 8], BF16)
        mbc = sb("mbc", [128, 12], F32)
        gsc_sb = sb("gsc_sb", [128, 64], F32)
        usc_sb = sb("usc_sb", [128, 64], F32)
        dsc_sb = sb("dsc_sb", [128, 16], F32)
        dscn_sb = sb("dscn_sb", [128, 16], F32)
        ident_sb = sb("ident_sb", [128, 128], F32)
        ao_sel = sb("ao_sel", [128, 16 * 16], BF16)
        bo_sel = sb("bo_sel", [16, D], BF16)
        bn6 = sb("bn6", [128, 24], F32)
        mv_sb = sb("mv_sb", [128, 4], F32)
        eps_sb = sb("eps_sb", [128, 1], F32)
        zro_sb = sb("zro_sb", [128, 1], F32)

        def s(nm):
            return Sem(ctx, nc, nm)

        sld = s("sld"); sldb = s("sldb"); sself = s("sself")
        sxl = s("sxl"); scc = s("scc")
        sst = s("sst"); smv = s("smv"); ssq = s("ssq"); sln = s("sln")
        sxst = s("sxst"); ssel = s("ssel"); sbc0 = s("sbc0"); smsel = s("smsel")
        sxnt = s("sxnt"); slowm = s("slowm"); slowe = s("slowe")
        sqkm = s("sqkm"); svm = s("svm"); sqke = s("sqke"); svve = s("svve")
        srope = s("srope"); sqkv = s("sqkv"); scsl = s("scsl")
        sqn = s("sqn"); sqne = s("sqne"); ssqr = s("ssqr"); scq = s("scq")
        snegc = s("snegc"); sbcc = s("sbcc"); scbc = s("scbc")
        ssmm = s("ssmm"); sexp = s("sexp"); smsk = s("smsk"); sav = s("sav")
        snrm = s("snrm"); satst = s("satst")
        ssmm2 = s("ssmm2"); sexp2 = s("sexp2"); smsk2 = s("smsk2")
        sav2 = s("sav2"); snrm2 = s("snrm2"); satst2 = s("satst2")
        satt = s("satt"); sxnm = s("sxnm"); sgul = s("sgul"); swdl = s("swdl")
        swol = s("swol"); sgum = s("sgum"); ssil = s("ssil"); sue = s("sue")
        shh = s("shh"); scsm = s("scsm"); scse = s("scse"); sbcs = s("sbcs")
        scsb = s("scsb"); sdnm = s("sdnm"); sdev = s("sdev")
        slom = s("slom"); sloe = s("sloe"); swom = s("swom"); swoe = s("swoe")
        strp = s("strp"); syv = s("syv"); syst = s("syst")
        scor = s("scor"); sdbg = s("sdbg")
        swcl = s("swcl"); swoc = s("swoc"); swst = s("swst")

        psSA = ctx.enter_context(nc.psum_tensor("psSA", [128, 1024], F32))
        psAV = [ctx.enter_context(nc.psum_tensor(f"psAV{i}", [128, 512], F32))
                for i in range(4)]
        psSB = ctx.enter_context(nc.psum_tensor("psSB", [128, 1024], F32))
        ps = [psSA[:, 0:512], psSA[:, 512:1024], psAV[0][:, :], psAV[1][:, :],
              psAV[2][:, :], psAV[3][:, :], psSB[:, 0:512], psSB[:, 512:1024]]
        PSS = [psSA, psSB]

        blk = ctx.enter_context(nc.Block())
        MM = nc.tensor.matmul

        ctx0 = ExitStack()

        def sb0(nm, shp, dt):
            return ctx0.enter_context(nc.sbuf_tensor(nm, shp, dt))

        wa_sb = sb0("wa_sb", [128, 16 * NL * 96], BF16)
        wb_sb = sb0("wb_sb", [96, NL * 256], BF16)
        wao_sb = sb0("wao_sb", [128, 16 * NL * 16], BF16)
        wbo_sb = sb0("wbo_sb", [32, NL * D], BF16)
        x0_sb = sb0("x0_sb", [128, 2 * D], F32)
        xn_sb = sb0("xn_sb", [128, 2 * D], BF16)
        wocv = sb0("wocv", [128, 2 * D], F32)
        wocb = sb0("wocb", [128, 2 * D], BF16)

        @blk.gpsimd
        def _(g):
            g.memset(ones_col[:, :], 1.0)
            g.memset(ones_f32[:, :], 1.0)
            g.memset(ones_rbf[:, :], 1.0)
            g.memset(eps_sb[:, :], 1e-5)
            g.memset(zro_sb[:, :], 0.0)
            sld.inc(g.dma_start(
                out=wa_sb[:, :].rearrange("p (c o) -> p c o", c=16),
                in_=waT.rearrange("(c p) o -> p c o", p=128)), 16)
            sld.inc(g.dma_start(out=wb_sb[:, :], in_=wbT), 16)
            sld.inc(g.dma_start(
                out=wao_sb[:, :].rearrange("p (c o) -> p c o", c=16),
                in_=waoT.rearrange("(c p) o -> p c o", p=128)), 16)
            sld.inc(g.dma_start(out=wbo_sb[:, :], in_=wboT), 16)
            sld.inc(g.dma_start(out=gsc_sb[:, :], in_=gsc_d), 16)
            sld.inc(g.dma_start(out=usc_sb[:, :], in_=usc_d), 16)
            sld.inc(g.dma_start(out=dsc_sb[:, :], in_=dsc_d), 16)
            sld.inc(g.dma_start(out=ident_sb[:, :], in_=ident_d), 16)
            sld.inc(g.dma_start(out=mrow[0:1, 0:4], in_=starts_d), 16)
            sld.inc(g.dma_start(out=mrow[0:1, 8:12], in_=iota4_d), 16)
            sld.inc(g.dma_start(out=mrow[0:1, 16:18], in_=bsel_d), 16)
            for os_ in range(16):
                if os_ >= 2:
                    g.wait_ge(swoc.h, os_ - 1)
                swcl.inc(g.dma_start(out=wocv[:, (os_ % 2) * D:(os_ % 2 + 1) * D],
                                     in_=woT_t[os_, :, :]), 16)

        @blk.scalar
        def _(sc):
            for os_ in range(16):
                sc.wait_ge(swcl.h, 16 * (os_ + 1))
                if os_ >= 2:
                    sc.wait_ge(swst.h, 16 * (os_ - 1))
                ii = nc.scalar.activation(wocb[:, (os_ % 2) * D:(os_ % 2 + 1) * D],
                                          wocv[:, (os_ % 2) * D:(os_ % 2 + 1) * D],
                                          AF.Copy)
                swoc.inc(ii, 1)
            sxl.inc(sc.dma_start(out=x0_sb[:, 0:D], in_=x_rows[0:128, :]), 16)
            sxl.inc(sc.dma_start(out=x0_sb[:, D:2 * D], in_=x_rows[128:256, :]), 16)
            sc.wait_ge(sld.h, 16)
            for t in range(8):
                sc.wait_ge(smv.h, t + 1)
                ii = nc.scalar.activation(mv_sb[:, 2:3], mv_sb[:, 1:2], AF.Sqrt,
                                          bias=eps_sb[:, 0:1], scale=1.0)
                ssq.inc(ii, 1)
                if t + 2 < 8:
                    sc.wait_ge(sln.h, t + 1)
                    sxl.inc(sc.dma_start(
                        out=x0_sb[:, ((t + 2) % 2) * D:((t + 2) % 2 + 1) * D],
                        in_=x_rows[(t + 2) * 128:(t + 3) * 128, :]), 16)

        @blk.vector
        def _(v):
            for t in range(8):
                v.wait_ge(sxl.h, 16 * (t + 1))
                if t >= 1:
                    v.wait_ge(ssq.h, t)
                xt = x0_sb[:, (t % 2) * D:(t % 2 + 1) * D]
                for j in range(4):
                    ii = nc.vector.bn_stats(bn6[:, j * 6:(j + 1) * 6],
                                            xt[:, j * 512:(j + 1) * 512])
                sst.inc(ii, 1)
                v.wait_ge(sst.h, sst.n)
                ii = nc.vector.bn_aggr(mv_sb[:, 0:2], bn6[:, :])
                smv.inc(ii, 1)
                v.wait_ge(ssq.h, t + 1)
                nc.vector.reciprocal(mv_sb[:, 3:4], mv_sb[:, 2:3])
                ii = nc.vector.tensor_scalar(
                    xn_sb[:, (t % 2) * D:(t % 2 + 1) * D], xt,
                    mv_sb[:, 0:1], mv_sb[:, 3:4], ALU.subtract, ALU.mult)
                sln.inc(ii, 1)

        @blk.sync
        def _(sy):
            for os_ in range(16):
                sy.wait_ge(swoc.h, os_ + 1)
                swst.inc(sy.dma_start(out=wo_bf[os_, :, :],
                                      in_=wocb[:, (os_ % 2) * D:(os_ % 2 + 1) * D]), 16)
            for t in range(8):
                sy.wait_ge(sln.h, t + 1)
                sxst.inc(sy.dma_start(out=xn_mine[t * 128:(t + 1) * 128, :],
                                      in_=xn_sb[:, (t % 2) * D:(t % 2 + 1) * D]), 16)

        @blk.gpsimd
        def _(g):
            g.wait_ge(sxst.h, 16 * 4)
            scc.inc(nc.gpsimd.collective_compute(
                "AllGather", ALU.bypass, replica_groups=RG,
                ins=[xn_mine[0:512, :].opt()], outs=[xn_allA.opt()]), 1)
            g.wait_ge(sxst.h, 16 * 8)
            scc.inc(nc.gpsimd.collective_compute(
                "AllGather", ALU.bypass, replica_groups=RG,
                ins=[xn_mine[512:1024, :].opt()], outs=[xn_allB.opt()]), 1)

        @blk.vector
        def _(v):
            v.wait_ge(sld.h, sld.n)
            for b in range(2):
                ii = nc.vector.tensor_scalar(mrow[0:1, 24 + 4 * b:28 + 4 * b],
                                             mrow[0:1, 0:4], float(b), None,
                                             ALU.is_le)
                smsel.inc(ii, 1)
                v.wait_ge(smsel.h, smsel.n)
                ii = nc.vector.tensor_reduce(mrow[0:1, 32 + b:33 + b],
                                             mrow[0:1, 24 + 4 * b:28 + 4 * b],
                                             AX.X, ALU.add)
                smsel.inc(ii, 1)
                v.wait_ge(smsel.h, smsel.n)
                ii = nc.vector.tensor_scalar(mrow[0:1, 32 + b:33 + b],
                                             mrow[0:1, 32 + b:33 + b],
                                             1.0, 0.0, ALU.subtract, ALU.max)
                smsel.inc(ii, 1)
                v.wait_ge(smsel.h, smsel.n)
                ii = nc.vector.tensor_scalar(mrow[0:1, 32 + b:33 + b],
                                             mrow[0:1, 32 + b:33 + b],
                                             3.0, None, ALU.min)
                smsel.inc(ii, 1)
                v.wait_ge(smsel.h, smsel.n)
                ii = nc.vector.tensor_scalar(mrow[0:1, 36 + 4 * b:40 + 4 * b],
                                             mrow[0:1, 8:12],
                                             mrow[0:1, 32 + b:33 + b], None,
                                             ALU.is_equal)
                smsel.inc(ii, 1)
            v.wait_ge(smsel.h, smsel.n)
            ii = nc.vector.tensor_scalar(mrow[0:1, 44:48], mrow[0:1, 36:40],
                                         mrow[0:1, 16:17], None, ALU.mult)
            smsel.inc(ii, 1)
            v.wait_ge(smsel.h, smsel.n)
            ii = nc.vector.scalar_tensor_tensor(
                mrow[0:1, 44:48], mrow[0:1, 40:44], mrow[0:1, 17:18],
                mrow[0:1, 44:48], ALU.mult, ALU.add)
            smsel.inc(ii, 1)

        @blk.tensor
        def _(te):
            te.wait_ge(smsel.h, smsel.n)
            MM(ps[6][:, 0:4], ones_f32[0:1, :], mrow[0:1, 36:40],
               start=True, stop=True)
            MM(ps[6][:, 4:8], ones_f32[0:1, :], mrow[0:1, 40:44],
               start=True, stop=True, skip_group_check=True)
            ii = MM(ps[6][:, 8:12], ones_f32[0:1, :], mrow[0:1, 44:48],
                    start=True, stop=True, skip_group_check=True)
            sbc0.inc(ii, 1)

        @blk.vector
        def _(v):
            v.wait_ge(sbc0.h, 1)
            ii = nc.vector.tensor_copy(mbc[:, 0:12], ps[6][:, 0:12])
            smsel.inc(ii, 1)
            v.wait_ge(smsel.h, smsel.n)
            nc.vector.tensor_scalar(dscn_sb[:, :], dsc_sb[:, :], -128.0, None,
                                    ALU.mult)
            for dc in range(16):
                dst = ao_sel[:, dc * 16:(dc + 1) * 16]
                nc.vector.tensor_scalar(
                    dst, wao_sb[:, (dc * NL + 0) * 16:(dc * NL + 1) * 16],
                    mbc[:, 8:9], None, ALU.mult)
                for a in range(1, NL):
                    nc.vector.scalar_tensor_tensor(
                        dst, wao_sb[:, (dc * NL + a) * 16:(dc * NL + a + 1) * 16],
                        mbc[:, 8 + a:9 + a], dst, ALU.mult, ALU.add)
            nc.vector.tensor_scalar(bo_sel[:, :], wbo_sb[0:16, 0:D],
                                    mbc[0:16, 8:9], None, ALU.mult)
            for a in range(1, NL):
                ii = nc.vector.scalar_tensor_tensor(
                    bo_sel[:, :], wbo_sb[0:16, a * D:(a + 1) * D],
                    mbc[0:16, 8 + a:9 + a], bo_sel[:, :], ALU.mult, ALU.add)
            smsel.inc(ii, 1)

        ctx0.close()
        ctxBC = ExitStack()

        def sbc(nm, shp, dt):
            return ctxBC.enter_context(nc.sbuf_tensor(nm, shp, dt))

        q_res = sbc("q_res", [128, HBS * S], BF16)
        k_res = sbc("k_res", [128, HBS * S], BF16)
        v_res = sbc("v_res", [128, HBS * VHB], BF16)
        wqkv_sb = sbc("wqkv_sb", [128, 16 * 768], BF16)
        a_sel = sbc("a_sel", [128, 2 * 16 * 96], BF16)
        b_sel = sbc("b_sel", [96, 2 * 256], BF16)
        cmask_sb = sbc("cmask_sb", [128, 2048], BF16)
        xnT = sbc("xnT", [128, 2 * 16 * STILE], BF16)
        low_sb = sbc("low_sb", [96, 2 * STILE], BF16)
        qtmp = sbc("qtmp", [128, 4 * STILE], BF16)
        exp_sb = sbc("exp_sb", [128, 8 * STILE], BF16)
        at_st = sbc("at_st", [128, 4 * STILE], BF16)
        cosb = sbc("cosb", [128, 2 * STILE], F32)
        sinb = sbc("sinb", [128, 2 * STILE], F32)
        cbc_sb = sbc("cbc_sb", [128, 32], F32)
        TORD = [2 * i for i in range(8)] + [2 * i + 1 for i in range(8)]

        @blk.gpsimd
        def _(g):
            g.wait_ge(smsel.h, smsel.n)
            g.wait_ge(sxst.h, 16 * 8)
            for hb in range(HBS):
                for ch in range(32):
                    g.memset(v_res[:, hb * VHB + ch * VST + 128:
                                   hb * VHB + ch * VST + 129], 1.0)
            sldb.inc(g.dma_start(
                out=wqkv_sb[:, :].rearrange("p (c o) -> p c o", c=16),
                in_=wqkvT.rearrange("(c p) o -> p c o", p=128)), 16)
            sldb.inc(g.dma_start(out=cmask_sb[:, :], in_=cmask_d), 16)
            sself.inc(g.dma_start(
                out=xnT[:, 0:NL * 96 * 16].rearrange("p (c o) -> p c o", c=16),
                in_=waT.rearrange("(c p) o -> p c o", p=128)), 16)
            sself.inc(g.dma_start(out=low_sb[0:96, 0:1024], in_=wbT), 16)
            for ti, t in enumerate(TORD):
                if ti >= 2:
                    g.wait_ge(srope.h, 4 * (ti - 1))
                colc = (t % 8) * STILE
                scsl.inc(g.dma_start(out=cosb[:, (ti % 2) * STILE:(ti % 2 + 1) * STILE],
                                     in_=cos_d[:, colc:colc + STILE]), 16)
                scsl.inc(g.dma_start(out=sinb[:, (ti % 2) * STILE:(ti % 2 + 1) * STILE],
                                     in_=sin_d[:, colc:colc + STILE]), 16)

        @blk.vector
        def _(v):
            v.wait_ge(sself.h, sself.n)
            for b in range(2):
                for dc in range(16):
                    dst = a_sel[:, (b * 16 + dc) * 96:(b * 16 + dc + 1) * 96]
                    nc.vector.tensor_scalar(
                        dst, xnT[:, (dc * NL + 0) * 96:(dc * NL + 1) * 96],
                        mbc[:, 4 * b:4 * b + 1], None, ALU.mult)
                    for a in range(1, NL):
                        nc.vector.scalar_tensor_tensor(
                            dst, xnT[:, (dc * NL + a) * 96:(dc * NL + a + 1) * 96],
                            mbc[:, 4 * b + a:4 * b + a + 1], dst, ALU.mult, ALU.add)
                dstb = b_sel[:, b * 256:(b + 1) * 256]
                nc.vector.tensor_scalar(dstb, low_sb[0:96, 0:256],
                                        mbc[0:96, 4 * b:4 * b + 1], None, ALU.mult)
                for a in range(1, NL):
                    ii = nc.vector.scalar_tensor_tensor(
                        dstb, low_sb[0:96, a * 256:(a + 1) * 256],
                        mbc[0:96, 4 * b + a:4 * b + a + 1], dstb, ALU.mult, ALU.add)
            ssel.inc(ii, 1)

        def xnt_t(ti, dc):
            return xnT[:, ((ti % 2) * 16 + dc) * STILE:((ti % 2) * 16 + dc + 1) * STILE]

        @blk.sync
        def _(sy):
            sy.wait_ge(ssel.h, 1)
            for ti, t in enumerate(TORD):
                sy.wait_ge(scc.h, 1 if ti < 8 else 2)
                if ti >= 2:
                    sy.wait_ge(sqkv.h, ti - 1)
                r, half = t // 2, t % 2
                xsrc = xn_allA if half == 0 else xn_allB
                src = xsrc[r * STILE:(r + 1) * STILE, :]
                dst = xnT[:, (ti % 2) * 16 * STILE:((ti % 2) + 1) * 16 * STILE] \
                    .rearrange("p (c s) -> p c s", c=16)
                sxnt.inc(sy.dma_start(out=dst, in_=src, transpose=True), 16)

        @blk.tensor
        def _(te):
            te.wait_ge(sldb.h, sldb.n)
            te.wait_ge(ssel.h, 1)
            for ti, t in enumerate(TORD):
                b = t // 8
                te.wait_ge(sxnt.h, 16 * (ti + 1))
                if ti >= 1:
                    te.wait_ge(slowe.h, ti)
                for dc in range(16):
                    ii = MM(ps[6][0:96, 0:STILE],
                            a_sel[:, (b * 16 + dc) * 96:(b * 16 + dc + 1) * 96],
                            xnt_t(ti, dc), start=(dc == 0), stop=(dc == 15),
                            skip_group_check=True)
                slowm.inc(ii, 1)
                if ti >= 1:
                    te.wait_ge(sqke.h, 4 * ti)
                    te.wait_ge(svve.h, ti)
                for pi in range(4):
                    wcol = pi * 128 if pi < 2 else 256 + (pi - 2) * 128
                    for dc in range(16):
                        MM(ps[pi][:, 0:STILE],
                           wqkv_sb[:, dc * 768 + wcol: dc * 768 + wcol + 128],
                           xnt_t(ti, dc), start=(dc == 0), stop=False,
                           skip_group_check=True)
                for ss in range(4):
                    bank = 4 if ss < 2 else 5
                    off = (ss % 2) * 256
                    for dc in range(16):
                        MM(ps[bank][:, off:off + 256],
                           xnt_t(ti, dc)[:, ss * 128:(ss + 1) * 128],
                           wqkv_sb[:, dc * 768 + 512: dc * 768 + 768],
                           start=(dc == 0 and ss % 2 == 0), stop=False,
                           skip_group_check=True)
                te.wait_ge(slowe.h, ti + 1)
                for pi in range(4):
                    proj, hi = pi // 2, pi % 2
                    ii = MM(ps[pi][:, 0:STILE],
                            b_sel[proj * 32: proj * 32 + 16,
                                  b * 256 + hi * 128: b * 256 + (hi + 1) * 128],
                            low_sb[proj * 32: proj * 32 + 16,
                                   (ti % 2) * STILE:(ti % 2 + 1) * STILE],
                            start=False, stop=True, skip_group_check=True)
                    sqkm.inc(ii, 1)
                for ss in range(4):
                    bank = 4 if ss < 2 else 5
                    off = (ss % 2) * 256
                    ii = MM(ps[bank][:, off:off + 256],
                            low_sb[64:80, (ti % 2) * STILE + ss * 128:
                                   (ti % 2) * STILE + (ss + 1) * 128],
                            b_sel[64:80, b * 256:(b + 1) * 256],
                            start=False, stop=(ss % 2 == 1), skip_group_check=True)
                    if ss % 2 == 1:
                        svm.inc(ii, 1)
                sqkv.inc(te.nop(), 1)

        @blk.scalar
        def _(sc):
            for ti in range(NT):
                for pi in range(4):
                    sc.wait_ge(sqkm.h, 4 * ti + pi + 1)
                    if ti >= 1:
                        sc.wait_ge(srope.h, 4 * (ti - 1) + pi + 1)
                    ii = nc.scalar.activation(qtmp[:, pi * STILE:(pi + 1) * STILE],
                                              ps[pi][:, 0:STILE], AF.Copy)
                    sqke.inc(ii, 1)

        @blk.vector
        def _(v):
            for ti, t in enumerate(TORD):
                b = t // 8
                v.wait_ge(slowm.h, ti + 1)
                ii = nc.vector.tensor_copy(
                    low_sb[:, (ti % 2) * STILE:(ti % 2 + 1) * STILE],
                    ps[6][0:96, 0:STILE])
                slowe.inc(ii, 1)
                for ss in range(4):
                    v.wait_ge(svm.h, 2 * ti + 1 + (1 if ss >= 2 else 0))
                    bank = 4 if ss < 2 else 5
                    off = (ss % 2) * 256
                    ch = (t % 8) * 4 + ss
                    for hi in range(2):
                        hb = b * 2 + hi
                        ii = nc.vector.tensor_copy(
                            v_res[:, hb * VHB + ch * VST: hb * VHB + ch * VST + 128],
                            ps[bank][:, off + hi * 128: off + (hi + 1) * 128])
                svve.inc(ii, 1)
                for pi in range(4):
                    v.wait_ge(sqke.h, 4 * ti + pi + 1)
                    v.wait_ge(scsl.h, 32 * (ti + 1))
                    isq, hi = pi < 2, pi % 2
                    hb = b * 2 + hi
                    res = q_res if isq else k_res
                    slot = qtmp[:, pi * STILE:(pi + 1) * STILE]
                    colc = (t % 8) * STILE
                    cs_ = cosb[:, (ti % 2) * STILE:(ti % 2 + 1) * STILE]
                    sn_ = sinb[:, (ti % 2) * STILE:(ti % 2 + 1) * STILE]
                    dst = res[:, hb * S + colc: hb * S + colc + STILE]
                    x1, x2 = slot[0:64, :], slot[64:128, :]
                    w1 = at_st[0:64, 0:STILE]
                    w2 = at_st[64:128, 0:STILE]
                    nc.vector.tensor_tensor(w1, x2, sn_[64:128, :], ALU.mult)
                    nc.vector.tensor_tensor(w2, x1, sn_[0:64, :], ALU.mult)
                    nc.vector.tensor_tensor(dst[0:64, :], x1, cs_[0:64, :], ALU.mult)
                    nc.vector.tensor_tensor(dst[64:128, :], x2, cs_[64:128, :], ALU.mult)
                    nc.vector.tensor_tensor(dst[0:64, :], dst[0:64, :], w1,
                                            ALU.subtract)
                    ii = nc.vector.tensor_tensor(dst[64:128, :], dst[64:128, :], w2,
                                                 ALU.add)
                    srope.inc(ii, 1)

        LKC = [(qt, kc) for qt in range(QT) for kc in _kcs(qt)]
        L = len(LKC)
        QEND = {}
        for li, (qt, kc) in enumerate(LKC):
            QEND[qt] = li
        DRANK = {}
        _dcnt = [0, 0]
        for p_ in range(2):
            for li_ in range(L):
                for ss_ in range(2):
                    qt_, kc_ = LKC[li_]
                    if kc_ >= 4 * qt_:
                        DRANK[(p_, ss_, li_)] = _dcnt[ss_]
                        _dcnt[ss_] += 1
        SSMM = [ssmm, ssmm2]
        SEXP = [sexp, sexp2]
        SMSK = [smsk, smsk2]
        SAV = [sav, sav2]
        SNRM = [snrm, snrm2]
        SATST = [satst, satst2]

        n2q = xnT[0:1, 16 * STILE:16 * STILE + S]
        n2k = xnT[0:1, 16 * STILE + S:16 * STILE + 2 * S]

        @blk.scalar
        def _(sc):
            for hb in range(HBS):
                sc.wait_ge(scq.h, 2 * hb + 2)
                ii = nc.scalar.activation(n2q, n2q, AF.Sqrt,
                                          bias=zro_sb[0:1, 0:1], scale=1.0 / 128.0)
                ssqr.inc(ii, 1)

        @blk.tensor
        def _(te):
            te.wait_ge(svve.h, NT)
            te.wait_ge(slowe.h, NT)
            te.wait_ge(sqke.h, 4 * NT)
            te.wait_ge(srope.h, 4 * NT)
            for hb in range(HBS):
                for j in range(16):
                    te.wait_ge(ssqr.h, 3 * hb + (1 if j < 8 else 2))
                    if j == 0 and hb >= 1:
                        te.wait_ge(scbc.h, hb)
                    if j >= 2:
                        te.wait_ge(sqne.h, 16 * hb + j - 1)
                    ii = MM(ps[6 + j % 2][0:1, 0:STILE], ones_col[:, 0:1],
                            xnT[:, j * 512:(j + 1) * 512],
                            start=True, stop=True, skip_group_check=True)
                    sqn.inc(ii, 1)
                te.wait_ge(snegc.h, 2 * hb + 2)
                ii = MM(ps[6][:, 0:8], ones_rbf[0:1, :], cqt_bf[0:1, 0:8],
                        start=True, stop=True, skip_group_check=True)
                sbcc.inc(ii, 1)

        @blk.vector
        def _(v):
            for hb in range(HBS):
                if hb >= 1:
                    v.wait_ge(sqn.h, 16 * hb)
                ii = nc.vector.tensor_tensor(xnT[:, 0:S],
                                             q_res[:, hb * S:(hb + 1) * S],
                                             q_res[:, hb * S:(hb + 1) * S], ALU.mult)
                ssqr.inc(ii, 1)
                ii = nc.vector.tensor_tensor(xnT[:, S:2 * S],
                                             k_res[:, hb * S:(hb + 1) * S],
                                             k_res[:, hb * S:(hb + 1) * S], ALU.mult)
                ssqr.inc(ii, 1)
                for j in range(16):
                    v.wait_ge(sqn.h, 16 * hb + j + 1)
                    dst = n2q if j < 8 else n2k
                    jj = j % 8
                    ii = nc.vector.tensor_copy(dst[0:1, jj * 512:(jj + 1) * 512],
                                               ps[6 + j % 2][0:1, 0:STILE])
                    sqne.inc(ii, 1)
                v.wait_ge(sqne.h, sqne.n)
                ii = nc.vector.tensor_reduce(mrow[0:1, 48:49], n2k, AX.X, ALU.max)
                scq.inc(ii, 1)
                v.wait_ge(scq.h, scq.n)
                ii = nc.vector.tensor_scalar(n2q, n2q, mrow[0:1, 48:49], None,
                                             ALU.mult)
                scq.inc(ii, 1)
                v.wait_ge(ssqr.h, 3 * hb + 3)
                ii = nc.vector.tensor_reduce(
                    mrow[0:1, 56:64],
                    n2q.rearrange("o (a x) -> o a x", a=8), AX.X, ALU.max)
                snegc.inc(ii, 1)
                v.wait_ge(snegc.h, snegc.n)
                ii = nc.vector.tensor_scalar(cqt_bf[0:1, 0:8], mrow[0:1, 56:64],
                                             -1.0, None, ALU.mult)
                snegc.inc(ii, 1)
                v.wait_ge(sbcc.h, hb + 1)
                ii = nc.vector.tensor_copy(cbc_sb[:, hb * 8:(hb + 1) * 8],
                                           ps[6][:, 0:8])
                scbc.inc(ii, 1)

        @blk.tensor
        def _(te):
            te.wait_ge(scbc.h, 4)

            def emit_av(p, ss, lj):
                hb = 2 * p + ss
                qt2, kc2 = LKC[lj]
                jpj = p * (L // 2) + lj // 2
                abanks = (2, 3) if ss == 0 else (4, 5)
                if kc2 >= 4 * qt2:
                    te.wait_ge(SMSK[ss].h, DRANK[(p, ss, lj)] + 1)
                else:
                    te.wait_ge(SEXP[ss].h, jpj + 1)
                if kc2 == 0:
                    te.wait_ge(SNRM[ss].h, 8 * p + qt2)
                slot = exp_sb[:, (ss * 4 + (lj // 2 % 2) * 2 + lj % 2) * STILE:
                              (ss * 4 + (lj // 2 % 2) * 2 + lj % 2 + 1) * STILE]
                for qs in range(4):
                    bank = abanks[qs // 2]
                    off = (qs % 2) * 256
                    ii = MM(ps[bank][:, off:off + 129],
                            slot[:, qs * 128:(qs + 1) * 128],
                            v_res[:, hb * VHB + kc2 * VST: hb * VHB + kc2 * VST + 129],
                            start=(kc2 == 0 and qs % 2 == 0), stop=False,
                            skip_group_check=True)
                SAV[ss].inc(ii, 1)

            for p in range(2):
                for li in range(L):
                    for ss in range(2):
                        hb = 2 * p + ss
                        qt, kc = LKC[li]
                        gli = p * L + li
                        jp = p * (L // 2) + li // 2
                        if li % 2 == 0 and jp >= 1:
                            te.wait_ge(SEXP[ss].h, jp)
                        ii = MM(PSS[ss][:, (li % 2) * 512:(li % 2 + 1) * 512],
                                k_res[:, hb * S + kc * 128: hb * S + (kc + 1) * 128],
                                q_res[:, hb * S + qt * STILE: hb * S + (qt + 1) * STILE],
                                start=True, stop=True, skip_group_check=True)
                        SSMM[ss].inc(ii, 1)
                        if li >= 1:
                            emit_av(p, ss, li - 1)
                emit_av(p, 0, L - 1)
                emit_av(p, 1, L - 1)

        @blk.scalar
        def _(sc):
            for p in range(2):
                sc.wait_ge(scbc.h, 2 * p + 2)
                for li in range(L):
                    if li % 2 == 0:
                        continue
                    j = li // 2
                    for ss in range(2):
                        hb = 2 * p + ss
                        qt, kc = LKC[li]
                        gli = p * L + li
                        sc.wait_ge(SSMM[ss].h, gli + 1)
                        if j >= 2:
                            sc.wait_ge(SAV[ss].h, p * L + 2 * j - 2)
                        ii = nc.scalar.activation(
                            exp_sb[:, (ss * 4 + (j % 2) * 2) * STILE:
                                   (ss * 4 + (j % 2) * 2 + 2) * STILE],
                            PSS[ss][:, 0:1024], AF.Exp,
                            bias=cbc_sb[:, hb * 8 + qt: hb * 8 + qt + 1],
                            scale=INV_SQRT_DK)
                        SEXP[ss].inc(ii, 1)

        @blk.vector
        def _(v):
            for p in range(2):
                for li in range(L):
                    for ss in range(2):
                        qt, kc = LKC[li]
                        gli = p * L + li
                        jp = p * (L // 2) + li // 2
                        if kc >= 4 * qt:
                            v.wait_ge(SEXP[ss].h, jp + 1)
                            voff = (kc - 4 * qt) * STILE
                            slot = exp_sb[:, (ss * 4 + (li // 2 % 2) * 2 + li % 2) * STILE:
                                          (ss * 4 + (li // 2 % 2) * 2 + li % 2 + 1) * STILE]
                            ii = nc.vector.tensor_tensor(
                                slot, slot, cmask_sb[:, voff:voff + STILE], ALU.mult)
                            SMSK[ss].inc(ii, 1)
                        if li == QEND[qt]:
                            gqt = 8 * p + qt
                            v.wait_ge(SAV[ss].h, p * L + li + 1)
                            if gqt >= 2:
                                v.wait_ge(SATST[ss].h, 16 * (gqt - 1))
                            abanks = (2, 3) if ss == 0 else (4, 5)
                            stg = at_st[:, (ss * 2 + qt % 2) * STILE:
                                        (ss * 2 + qt % 2 + 1) * STILE]
                            for qs in range(4):
                                bank = abanks[qs // 2]
                                off = (qs % 2) * 256
                                ii = nc.vector.reciprocal(
                                    small[:, qs:qs + 1],
                                    ps[bank][:, off + 128:off + 129])
                            smsel.inc(ii, 1)
                            v.wait_ge(smsel.h, smsel.n)
                            for qs in range(4):
                                bank = abanks[qs // 2]
                                off = (qs % 2) * 256
                                ii = nc.vector.tensor_scalar(
                                    stg[:, qs * 128:(qs + 1) * 128],
                                    ps[bank][:, off:off + 128],
                                    small[:, qs:qs + 1], None, ALU.mult)
                            SNRM[ss].inc(ii, 1)

        @blk.sync
        def _(sy):
            for p in range(2):
                for qt in range(QT):
                    for ss in range(2):
                        hb = 2 * p + ss
                        b, hi = hb // 2, hb % 2
                        gqt = 8 * p + qt
                        sy.wait_ge(SNRM[ss].h, gqt + 1)
                        stg = at_st[:, (ss * 2 + qt % 2) * STILE:
                                    (ss * 2 + qt % 2 + 1) * STILE]
                        dst = at_mine[b * S + qt * STILE: b * S + (qt + 1) * STILE,
                                      hi * 128:(hi + 1) * 128]
                        SATST[ss].inc(sy.dma_start(
                            out=dst.rearrange("(a p) d -> p a d", p=128),
                            in_=stg.rearrange("p (a d) -> p a d", a=4)), 16)

        @blk.gpsimd
        def _(g):
            g.wait_ge(satst.h, 16 * 8)
            g.wait_ge(satst2.h, 16 * 8)
            scc.inc(nc.gpsimd.collective_compute(
                "AllGather", ALU.bypass, replica_groups=RG,
                ins=[at_mine[0:S, :].opt()], outs=[at_allA.opt()]), 1)
            g.wait_ge(satst.h, 16 * 16)
            g.wait_ge(satst2.h, 16 * 16)
            scc.inc(nc.gpsimd.collective_compute(
                "AllGather", ALU.bypass, replica_groups=RG,
                ins=[at_mine[S:B * S, :].opt()], outs=[at_allB.opt()]), 1)

        if DEBUG >= 1:
            @blk.gpsimd
            def _(g):
                g.wait_ge(scc.h, 2)
                sdbg.inc(g.dma_start(out=dbg["xn"][0:NC * 512, :], in_=xn_allA), 16)
                sdbg.inc(g.dma_start(out=dbg["xn"][NC * 512:, :], in_=xn_allB), 16)
                g.wait_ge(sqkv.h, NT)
                g.wait_ge(srope.h, 4 * NT)
                g.wait_ge(svve.h, NT)
                sdbg.inc(g.dma_start(out=dbg["q"], in_=q_res[:, :]), 16)
                sdbg.inc(g.dma_start(out=dbg["k"], in_=k_res[:, :]), 16)
                sdbg.inc(g.dma_start(out=dbg["v"], in_=v_res[:, :]), 16)
                g.wait_ge(slowe.h, NT)
                sdbg.inc(g.dma_start(out=dbg["low"], in_=low_sb[:, :]), 16)
        if DEBUG >= 2:
            @blk.gpsimd
            def _(g):
                g.wait_ge(scc.h, 4)
                sdbg.inc(g.dma_start(out=dbg["at"], in_=at_mine), 16)
                g.wait_ge(scbc.h, 4)
                sdbg.inc(g.dma_start(out=dbg["cbc"], in_=cbc_sb[:, :]), 16)

        ctxBC.close()

        attnT = sb("attnT", [128, 16 * GR], BF16)
        acc = sb("acc", [128, 16 * GR], F32)
        xnTm = sb("xnTm", [128, 16 * GR], BF16)
        wg_sb = sb("wg_sb", [128, 2 * 4096], BF16)
        wu_sb = sb("wu_sb", [128, 2 * 4096], BF16)
        wd_sb = sb("wd_sb", [128, 3 * 4096], BF16)
        wo_sb = sb("wo_sb", [128, 2 * D], BF16)
        h_sb = sb("h_sb", [128, 32 * GR], BF16)
        sgu_t = sb("sgu_t", [128, 2 * GR], BF16)
        ut_t = sb("ut_t", [128, 2 * GR], BF16)
        low_o = sb("low_o", [16, GR], BF16)
        cs_bc = sb("cs_bc", [128, GR], F32)
        csn_r = sb("csn_r", [1, GR], F32)
        y_st = sb("y_st", [128, D], F32)
        xE_sb = sb("xE_sb", [128, 2 * D], F32)

        @blk.gpsimd
        def _(g):
            g.wait_ge(satst.h, 16 * 16)
            g.wait_ge(satst2.h, 16 * 16)
            scc.inc(nc.gpsimd.collective_compute(
                "AllGather", ALU.bypass, replica_groups=RG,
                ins=[at_mine[S:B * S, :].opt()], outs=[at_allB.opt()]), 1)
            for grp in range(GRP):
                for ihalf in range(IH):
                    for sl in range(NSLAB):
                        k = (grp * IH + ihalf) * NSLAB + sl
                        if k >= 2:
                            g.wait_ge(sgum.h, 4 * (k - 1))
                        sgul.inc(g.dma_start(
                            out=wg_sb[:, (k % 2) * 4096:(k % 2 + 1) * 4096],
                            in_=g_u8[ihalf * NSLAB + sl, :, :]), 16)
                        sgul.inc(g.dma_start(
                            out=wu_sb[:, (k % 2) * 4096:(k % 2 + 1) * 4096],
                            in_=u_u8[ihalf * NSLAB + sl, :, :]), 16)
                    for ms in range(16):
                        w = (grp * IH + ihalf) * 16 + ms
                        if w >= 3:
                            g.wait_ge(sdnm.h, w - 2)
                        swdl.inc(g.dma_start(
                            out=wd_sb[:, (w % 3) * 4096:(w % 3 + 1) * 4096],
                            in_=d_u8[ms, :, ihalf * 4096:(ihalf + 1) * 4096]), 16)
                        if ihalf == 1:
                            k2 = grp * 16 + ms
                            if k2 >= 2:
                                g.wait_ge(swom.h, k2 - 1)
                            swol.inc(g.dma_start(
                                out=wo_sb[:, (k2 % 2) * D:(k2 % 2 + 1) * D],
                                in_=wo_bf[ms, :, :]), 16)

        @blk.sync
        def _(sy):
            sy.wait_ge(satst.h, 16 * 16)
            sy.wait_ge(satst2.h, 16 * 16)
            for grp in range(GRP):
                if grp >= 1:
                    sy.wait_ge(sgum.h, 4 * NSLAB * IH * grp)
                    sy.wait_ge(swom.h, 16 * grp)
                src = xn_mine[grp * GR:(grp + 1) * GR, :]
                sxnm.inc(sy.dma_start(
                    out=xnTm[:, :].rearrange("p (c s) -> p c s", c=16),
                    in_=src, transpose=True), 16)
                for arm in nc.sync.switch_core_id(n=128):
                    rank = arm.logical % 8
                    b, sq = rank // 4, (rank % 4) * ROWS
                    nc.sync.wait_ge(scc.h, 3 if b == 0 else 4)
                    at_src = at_allA if b == 0 else at_allB
                    for r in range(8):
                        src2 = at_src[r * S + sq + grp * GR:
                                      r * S + sq + (grp + 1) * GR, :]
                        dst2 = attnT[:, (2 * r) * GR:(2 * r + 2) * GR] \
                            .rearrange("p (hl s) -> p hl s", hl=2)
                        sy.dma_start(out=dst2, in_=src2,
                                     transpose=True).then_inc(satt.h, 16)
                satt.n += 8 * 16
                for ss in range(4):
                    gss = grp * 4 + ss
                    sy.wait_ge(syv.h, 16 * (gss + 1))
                    syst.inc(sy.dma_start(
                        out=y_out[grp * GR + ss * 128: grp * GR + (ss + 1) * 128, :],
                        in_=y_st[:, :]), 16)
            sy.wait_ge(syst.h, syst.n)

        @blk.scalar
        def _(sc):
            for grp in range(GRP):
                for ihalf in range(IH):
                    for isl in range(32):
                        gi = (grp * IH + ihalf) * 32 + isl
                        sc.wait_ge(sgum.h, 2 * gi + 1)
                        if gi >= 2:
                            sc.wait_ge(shh.h, gi - 1)
                        ii = nc.scalar.activation(
                            sgu_t[:, (isl % 2) * GR:(isl % 2 + 1) * GR],
                            ps[2 * (isl % 2)][:, 0:GR], AF.Silu,
                            scale=gsc_sb[:, ihalf * 32 + isl:ihalf * 32 + isl + 1])
                        ssil.inc(ii, 1)
                for ss in range(4):
                    gss = grp * 4 + ss
                    if gss >= 2:
                        sc.wait_ge(syv.h, 16 * (gss - 1))
                    sxl.inc(sc.dma_start(
                        out=xE_sb[:, (gss % 2) * D:(gss % 2 + 1) * D],
                        in_=x_rows[grp * GR + ss * 128: grp * GR + (ss + 1) * 128, :]), 16)

        @blk.tensor
        def _(te):
            te.wait_ge(sav.h, 2 * L)
            te.wait_ge(sav2.h, 2 * L)
            te.wait_ge(snrm.h, 16)
            te.wait_ge(snrm2.h, 16)
            te.wait_ge(sqne.h, 64)
            te.wait_ge(scbc.h, 4)
            for grp in range(GRP):
                te.wait_ge(sxnm.h, 16 * (grp + 1))
                te.wait_ge(syv.h, 64 * grp)
                te.wait_ge(ssil.h, 64 * grp)
                te.wait_ge(sue.h, 64 * grp)
                te.wait_ge(swoe.h, 16 * grp)
                te.wait_ge(sdev.h, 32 * grp)
                te.wait_ge(satt.h, 8 * 16 * (grp + 1))
                if grp >= 1:
                    te.wait_ge(sloe.h, grp)
                for dc in range(16):
                    ii = MM(ps[7][0:16, 0:GR], ao_sel[:, dc * 16:(dc + 1) * 16],
                            attnT[:, dc * GR:(dc + 1) * GR],
                            start=(dc == 0), stop=(dc == 15), skip_group_check=True)
                slom.inc(ii, 1)
                for ihalf in range(IH):
                    for isl in range(32):
                        sl, io = isl // 2, isl % 2
                        k = (grp * IH + ihalf) * NSLAB + sl
                        gi = (grp * IH + ihalf) * 32 + isl
                        te.wait_ge(sgul.h, 32 * (k + 1))
                        gb = 2 * (isl % 2)
                        ub = gb + 1
                        if isl >= 2 or (grp * IH + ihalf) >= 1:
                            te.wait_ge(ssil.h, gi - 1)
                            te.wait_ge(sue.h, gi - 1)
                        for dc in range(16):
                            MM(ps[gb][:, 0:GR],
                               wg_sb[:, (k % 2) * 4096 + dc * 256 + io * 128:
                                     (k % 2) * 4096 + dc * 256 + io * 128 + 128],
                               xnTm[:, dc * GR:(dc + 1) * GR],
                               start=(dc == 0), stop=(dc == 15),
                               skip_group_check=True)
                        sgum.inc(MM(ps[ub][:, 0:GR],
                                    wu_sb[:, (k % 2) * 4096 + io * 128:
                                          (k % 2) * 4096 + io * 128 + 128],
                                    xnTm[:, 0:GR], start=True, stop=False,
                                    skip_group_check=True), 1)
                        for dc in range(1, 16):
                            ii = MM(ps[ub][:, 0:GR],
                                    wu_sb[:, (k % 2) * 4096 + dc * 256 + io * 128:
                                          (k % 2) * 4096 + dc * 256 + io * 128 + 128],
                                    xnTm[:, dc * GR:(dc + 1) * GR],
                                    start=False, stop=(dc == 15),
                                    skip_group_check=True)
                        sgum.inc(ii, 1)
                    te.wait_ge(shh.h, (grp * IH + ihalf) * 32 + 32)
                    if ihalf == 0 and grp >= 1:
                        te.wait_ge(scsb.h, grp)
                    for icl in range(32):
                        ii = MM(ps[4][0:1, 0:GR], ones_col[:, 0:1],
                                h_sb[:, icl * GR:(icl + 1) * GR],
                                start=(ihalf == 0 and icl == 0),
                                stop=(ihalf == 1 and icl == 31),
                                skip_group_check=True)
                    scsm.inc(ii, 1)
                    for ms in range(16):
                        w = (grp * IH + ihalf) * 16 + ms
                        te.wait_ge(swdl.h, 16 * (w + 1))
                        db = 5 + ms % 2
                        if w >= 2:
                            te.wait_ge(sdev.h, w - 1)
                        for icl in range(32):
                            ii = MM(ps[db][:, 0:GR],
                                    wd_sb[:, (w % 3) * 4096 + icl * 128:
                                          (w % 3) * 4096 + icl * 128 + 128],
                                    h_sb[:, icl * GR:(icl + 1) * GR],
                                    start=(icl == 0), stop=(icl == 31),
                                    skip_group_check=True)
                        sdnm.inc(ii, 1)
                        if ihalf == 1:
                            k2 = grp * 16 + ms
                            te.wait_ge(swol.h, 16 * (k2 + 1))
                            wb_ = ms % 2
                            if ms == 0:
                                te.wait_ge(ssil.h, 64 * (grp + 1))
                                te.wait_ge(sue.h, 64 * (grp + 1))
                                te.wait_ge(sloe.h, grp + 1)
                            if ms >= 2:
                                te.wait_ge(swoe.h, grp * 16 + ms - 1)
                            for dc in range(16):
                                MM(ps[wb_][:, 0:GR],
                                   wo_sb[:, (k2 % 2) * D + dc * 128:
                                         (k2 % 2) * D + (dc + 1) * 128],
                                   attnT[:, dc * GR:(dc + 1) * GR],
                                   start=(dc == 0), stop=False,
                                   skip_group_check=True)
                            ii = MM(ps[wb_][:, 0:GR],
                                    bo_sel[:, ms * 128:(ms + 1) * 128],
                                    low_o[:, :], start=False, stop=True,
                                    skip_group_check=True)
                            swom.inc(ii, 1)
                te.wait_ge(scse.h, grp + 1)
                ii = MM(ps[4][:, 0:GR], ones_f32[0:1, :], csn_r[0:1, :],
                        start=True, stop=True, skip_group_check=True)
                sbcs.inc(ii, 1)
                for ss in range(4):
                    for ms in range(16):
                        ti = (grp * 4 + ss) * 16 + ms
                        tb = 2 + ti % 4
                        te.wait_ge(swoe.h, 16 * (grp + 1))
                        te.wait_ge(scor.h, 16 * (grp + 1))
                        if ti >= 4:
                            te.wait_ge(syv.h, ti - 3)
                        ii = nc.tensor.transpose(
                            ps[tb][:, 0:128],
                            acc[:, ms * GR + ss * 128: ms * GR + ss * 128 + 128],
                            ident_sb[:, :])
                        strp.inc(ii, 1)

        @blk.vector
        def _(v):
            for grp in range(GRP):
                v.wait_ge(slom.h, grp + 1)
                ii = nc.vector.tensor_copy(low_o[:, :], ps[7][0:16, 0:GR])
                sloe.inc(ii, 1)
                for ihalf in range(IH):
                    for isl in range(32):
                        gi = (grp * IH + ihalf) * 32 + isl
                        v.wait_ge(sgum.h, 2 * gi + 2)
                        if gi >= 2:
                            v.wait_ge(shh.h, gi - 1)
                        ub = 2 * (isl % 2) + 1
                        ii = nc.vector.tensor_scalar(
                            ut_t[:, (isl % 2) * GR:(isl % 2 + 1) * GR],
                            ps[ub][:, 0:GR],
                            usc_sb[:, ihalf * 32 + isl:ihalf * 32 + isl + 1],
                            None, ALU.mult)
                        sue.inc(ii, 1)
                        v.wait_ge(ssil.h, gi + 1)
                        ii = nc.vector.tensor_tensor(
                            h_sb[:, isl * GR:(isl + 1) * GR],
                            sgu_t[:, (isl % 2) * GR:(isl % 2 + 1) * GR],
                            ut_t[:, (isl % 2) * GR:(isl % 2 + 1) * GR], ALU.mult)
                        shh.inc(ii, 1)
                    for ms in range(16):
                        w = (grp * IH + ihalf) * 16 + ms
                        v.wait_ge(sdnm.h, w + 1)
                        aslice = acc[:, ms * GR:(ms + 1) * GR]
                        db = 5 + ms % 2
                        if ihalf == 0:
                            ii = nc.vector.tensor_scalar(
                                aslice, ps[db][:, 0:GR],
                                dsc_sb[:, ms:ms + 1], None, ALU.mult)
                        else:
                            ii = nc.vector.scalar_tensor_tensor(
                                aslice, ps[db][:, 0:GR], dsc_sb[:, ms:ms + 1],
                                aslice, ALU.mult, ALU.add)
                        sdev.inc(ii, 1)
                        if ihalf == 1:
                            v.wait_ge(swom.h, grp * 16 + ms + 1)
                            wb_ = ms % 2
                            ii = nc.vector.tensor_tensor(
                                aslice, aslice, ps[wb_][:, 0:GR], ALU.add)
                            swoe.inc(ii, 1)
                v.wait_ge(scsm.h, 2 * (grp + 1))
                ii = nc.vector.tensor_copy(csn_r[0:1, :], ps[4][0:1, 0:GR])
                scse.inc(ii, 1)
                v.wait_ge(sbcs.h, grp + 1)
                ii = nc.vector.tensor_copy(cs_bc[:, :], ps[4][:, 0:GR])
                scsb.inc(ii, 1)
                for ms in range(16):
                    v.wait_ge(swoe.h, grp * 16 + ms + 1)
                    aslice = acc[:, ms * GR:(ms + 1) * GR]
                    ii = nc.vector.scalar_tensor_tensor(
                        aslice, cs_bc[:, :], dscn_sb[:, ms:ms + 1], aslice,
                        ALU.mult, ALU.add)
                    scor.inc(ii, 1)
                for ss in range(4):
                    gss = grp * 4 + ss
                    v.wait_ge(sxl.h, 16 * (8 + gss + 1))
                    if gss >= 1:
                        v.wait_ge(syst.h, 16 * gss)
                    for ms in range(16):
                        ti = gss * 16 + ms
                        v.wait_ge(strp.h, ti + 1)
                        tb = 2 + ti % 4
                        ii = nc.vector.tensor_tensor(
                            y_st[:, ms * 128:(ms + 1) * 128],
                            ps[tb][:, 0:128],
                            xE_sb[:, (gss % 2) * D + ms * 128:
                                  (gss % 2) * D + (ms + 1) * 128], ALU.add)
                        syv.inc(ii, 1)

        if DEBUG >= 3:
            @blk.gpsimd
            def _(g):
                g.wait_ge(syst.h, 16 * 4 * GRP)
                sdbg.inc(g.dma_start(out=dbg["h"], in_=h_sb[:, :]), 16)
                sdbg.inc(g.dma_start(out=dbg["acc"], in_=acc[:, :]), 16)

        @blk.gpsimd
        def _(g):
            g.wait_ge(syst.h, 16 * 4 * GRP)
            if DEBUG >= 1:
                g.wait_ge(sdbg.h, sdbg.n)

    return nc


def kernel(**inputs):
    in_maps = _stage(inputs)
    nc = _build()
    res = run_bass_kernel_spmd(nc, in_maps, core_ids=list(range(NC)))
    y = np.concatenate([np.asarray(res.results[c]["y"], np.float32)
                        for c in range(NC)], axis=0)
    return y.reshape(B, S, D)


# version 25
# speedup vs baseline: 1.2363x; 1.2363x over previous
import numpy as np
import ml_dtypes
from contextlib import ExitStack

import concourse.bass as bass
import concourse.mybir as mybir
from concourse.bass_utils import run_bass_kernel_spmd

F32 = mybir.dt.float32
BF16 = mybir.dt.bfloat16
U8 = mybir.dt.uint8
I32 = mybir.dt.int32
AF = mybir.ActivationFunctionType
ALU = mybir.AluOpType
AX = mybir.AxisListType

DEBUG = 0

B = 2
S = 4096
D = 2048
DK = 128
NL = 4
NC = 8
ROWS = 1024
NT = 16
STILE = 512
QT = 8
HBS = 4
VST = 129
VHB = 32 * VST
GRP = 2
GR = ROWS // GRP
IH = 2
NSLAB = 16
INV_SQRT_DK = float(1.0 / np.sqrt(128.0))

_PERM = np.concatenate([np.arange(0, 128, 2), np.arange(1, 128, 2)])


def _stage(i):
    bf = ml_dtypes.bfloat16
    x = np.ascontiguousarray(np.asarray(i["L0_attn_x"], np.float32).reshape(B * S, D))

    Wq = np.asarray(i["L0_attn_Wq"], np.float32)
    Wk = np.asarray(i["L0_attn_Wk"], np.float32)
    Wv = np.asarray(i["L0_attn_Wv"], np.float32)
    Wo = np.asarray(i["L0_attn_Wo"], np.float32)
    Aq = np.asarray(i["L0_attn_Aq"], np.float32)
    Ak = np.asarray(i["L0_attn_Ak"], np.float32)
    Av = np.asarray(i["L0_attn_Av"], np.float32)
    Ao = np.asarray(i["L0_attn_Ao"], np.float32)
    Bq = np.asarray(i["L0_attn_Bq"], np.float32)
    Bk = np.asarray(i["L0_attn_Bk"], np.float32)
    Bv = np.asarray(i["L0_attn_Bv"], np.float32)
    Bo = np.asarray(i["L0_attn_Bo"], np.float32)

    waT = np.zeros((D, NL * 96), np.float32)
    for a in range(NL):
        waT[:, a * 96 + 0: a * 96 + 16] = Aq[a].T
        waT[:, a * 96 + 32: a * 96 + 48] = Ak[a].T
        waT[:, a * 96 + 64: a * 96 + 80] = Av[a].T
    waoT = np.zeros((D, NL * 16), np.float32)
    for a in range(NL):
        waoT[:, a * 16:(a + 1) * 16] = Ao[a].T
    wboT = np.zeros((32, NL * D), np.float32)
    for a in range(NL):
        wboT[0:16, a * D:(a + 1) * D] = Bo[a].T

    woT_t = np.ascontiguousarray(
        Wo.reshape(16, 128, 16, 128).transpose(0, 3, 2, 1)).reshape(16, 128, D)

    gw = np.asarray(i["L0_ffn_gate_w"])
    uw = np.asarray(i["L0_ffn_up_w"])
    dw = np.asarray(i["L0_ffn_down_w"])
    g_u8 = np.ascontiguousarray(
        gw.reshape(32, 256, 16, 128).transpose(0, 3, 2, 1)).astype(bf).reshape(32, 128, 4096)
    u_u8 = np.ascontiguousarray(
        uw.reshape(32, 256, 16, 128).transpose(0, 3, 2, 1)).astype(bf).reshape(32, 128, 4096)
    d_u8 = np.ascontiguousarray(
        dw.reshape(16, 128, 64, 128).transpose(0, 3, 2, 1)).astype(bf).reshape(16, 128, 8192)

    gsc = np.ascontiguousarray(np.asarray(i["L0_ffn_gate_scale"], np.float32).reshape(64, 128).T)
    usc = np.ascontiguousarray(np.asarray(i["L0_ffn_up_scale"], np.float32).reshape(64, 128).T)
    dsc = np.ascontiguousarray(np.asarray(i["L0_ffn_down_scale"], np.float32).reshape(16, 128).T)

    starts = np.asarray(i["L0_attn_lora_start_indices"], np.int32).reshape(1, NL)
    iota4 = np.arange(NL, dtype=np.float32).reshape(1, NL)

    inv_freq = 10000.0 ** (-np.arange(0, DK, 2, dtype=np.float64) / DK)
    ang = inv_freq[:, None] * np.arange(S, dtype=np.float64)[None, :]
    rope_cos = np.concatenate([np.cos(ang), np.cos(ang)], 0).astype(np.float32)
    rope_sin = np.concatenate([np.sin(ang), np.sin(ang)], 0).astype(np.float32)

    kk = np.arange(128)[:, None]
    qq = np.arange(512)[None, :]
    cmask = np.concatenate(
        [(qq >= kk + v * 128).astype(np.float32) for v in range(4)], axis=1
    ).astype(bf)
    ident = np.eye(128, dtype=np.float32)

    shared = {
        "waT": waT, "waoT": waoT, "wboT": wboT, "woT_t": woT_t,
        "g_u8": g_u8, "u_u8": u_u8, "d_u8": d_u8,
        "gsc": gsc, "usc": usc, "dsc": dsc,
        "starts": starts, "iota4": iota4,
        "rope_cos": rope_cos, "rope_sin": rope_sin,
        "cmask": np.ascontiguousarray(cmask), "ident": ident,
    }

    in_maps = []
    for c in range(NC):
        h0 = 2 * c
        cols = []
        for Wm, perm in ((Wq, True), (Wk, True), (Wv, False)):
            for hh in (h0, h0 + 1):
                rows = hh * DK + (_PERM if perm else np.arange(DK))
                cols.append(Wm[rows, :].T)
        wqkvT = np.ascontiguousarray(np.concatenate(cols, axis=1))
        wbT = np.zeros((96, NL * 256), np.float32)
        for a in range(NL):
            for pi, (Bm, perm) in enumerate(((Bq, True), (Bk, True), (Bv, False))):
                for hi, hh in enumerate((h0, h0 + 1)):
                    rows = hh * DK + (_PERM if perm else np.arange(DK))
                    wbT[pi * 32: pi * 32 + 16,
                        a * 256 + hi * 128: a * 256 + (hi + 1) * 128] = Bm[a][rows, :].T
        bsel = np.zeros((1, 2), np.float32)
        bsel[0, c // 4] = 1.0
        m = dict(shared)
        m.update({
            "x_rows": np.ascontiguousarray(x[c * ROWS:(c + 1) * ROWS]),
            "wqkvT": wqkvT,
            "wbT": np.ascontiguousarray(wbT),
            "bsel": bsel,
        })
        in_maps.append(m)
    return in_maps


class Sem:
    def __init__(self, ctx, nc, name):
        self.h = ctx.enter_context(nc.semaphore(name))
        self.n = 0

    def inc(self, instr, k=1):
        instr.then_inc(self.h, k)
        self.n += k
        return self.n


def _kcs(qt):
    return list(range(4 * qt + 4))


_ALL_IDX = []
for _hb in range(HBS):
    for _qt in range(QT):
        for _kc in _kcs(_qt):
            _ALL_IDX.append((_hb, _qt, _kc, _kc >= 4 * _qt))
IDX_OF = {(h, q, k): i for i, (h, q, k, d) in enumerate(_ALL_IDX)}
MASK_RANK = {}
for _i, _e in enumerate(_ALL_IDX):
    if _e[3]:
        MASK_RANK[_i] = len(MASK_RANK)
QT_END = {}
for _i, (_hb, _qt, _kc, _d) in enumerate(_ALL_IDX):
    QT_END[(_hb, _qt)] = _i
NIDX = len(_ALL_IDX)


def _build():
    nc = bass.Bass("TRN2", target_bir_lowering=False, debug=False)

    x_rows = nc.dram_tensor("x_rows", [ROWS, D], F32, kind="ExternalInput").ap()
    wqkvT = nc.dram_tensor("wqkvT", [D, 768], F32, kind="ExternalInput").ap()
    waT = nc.dram_tensor("waT", [D, NL * 96], F32, kind="ExternalInput").ap()
    wbT = nc.dram_tensor("wbT", [96, NL * 256], F32, kind="ExternalInput").ap()
    waoT = nc.dram_tensor("waoT", [D, NL * 16], F32, kind="ExternalInput").ap()
    wboT = nc.dram_tensor("wboT", [32, NL * D], F32, kind="ExternalInput").ap()
    woT_t = nc.dram_tensor("woT_t", [16, 128, D], F32, kind="ExternalInput").ap()
    g_u8 = nc.dram_tensor("g_u8", [IH * NSLAB, 128, 4096], BF16, kind="ExternalInput").ap()
    u_u8 = nc.dram_tensor("u_u8", [IH * NSLAB, 128, 4096], BF16, kind="ExternalInput").ap()
    d_u8 = nc.dram_tensor("d_u8", [16, 128, 8192], BF16, kind="ExternalInput").ap()
    gsc_d = nc.dram_tensor("gsc", [128, 64], F32, kind="ExternalInput").ap()
    usc_d = nc.dram_tensor("usc", [128, 64], F32, kind="ExternalInput").ap()
    dsc_d = nc.dram_tensor("dsc", [128, 16], F32, kind="ExternalInput").ap()
    starts_d = nc.dram_tensor("starts", [1, NL], I32, kind="ExternalInput").ap()
    iota4_d = nc.dram_tensor("iota4", [1, NL], F32, kind="ExternalInput").ap()
    cos_d = nc.dram_tensor("rope_cos", [128, S], F32, kind="ExternalInput").ap()
    sin_d = nc.dram_tensor("rope_sin", [128, S], F32, kind="ExternalInput").ap()
    cmask_d = nc.dram_tensor("cmask", [128, 2048], BF16, kind="ExternalInput").ap()
    ident_d = nc.dram_tensor("ident", [128, 128], F32, kind="ExternalInput").ap()
    bsel_d = nc.dram_tensor("bsel", [1, 2], F32, kind="ExternalInput").ap()

    y_out = nc.dram_tensor("y", [ROWS, D], F32, kind="ExternalOutput").ap()

    xn_mine = nc.dram_tensor("xn_mine", [ROWS, D], BF16, kind="Internal").ap()
    xn_allA = nc.dram_tensor("xn_allA", [NC * 512, D], BF16, kind="Internal",
                             addr_space="Shared").ap()
    xn_allB = nc.dram_tensor("xn_allB", [NC * 512, D], BF16, kind="Internal",
                             addr_space="Shared").ap()
    at_mine = nc.dram_tensor("at_mine", [B * S, 256], BF16, kind="Internal").ap()
    wo_bf = nc.dram_tensor("wo_bf", [16, 128, D], BF16, kind="Internal").ap()
    at_allA = nc.dram_tensor("at_allA", [NC * S, 256], BF16, kind="Internal",
                             addr_space="Shared").ap()
    at_allB = nc.dram_tensor("at_allB", [NC * S, 256], BF16, kind="Internal",
                             addr_space="Shared").ap()

    RG = [[0, 1, 2, 3, 4, 5, 6, 7]]

    dbg = {}
    if DEBUG >= 1:
        dbg["xn"] = nc.dram_tensor("dbg_xn", [NC * ROWS, D], BF16, kind="ExternalOutput").ap()
        dbg["q"] = nc.dram_tensor("dbg_q", [128, HBS * S], BF16, kind="ExternalOutput").ap()
        dbg["k"] = nc.dram_tensor("dbg_k", [128, HBS * S], BF16, kind="ExternalOutput").ap()
        dbg["v"] = nc.dram_tensor("dbg_v", [128, HBS * VHB], BF16, kind="ExternalOutput").ap()
        dbg["low"] = nc.dram_tensor("dbg_low", [96, 2 * STILE], BF16, kind="ExternalOutput").ap()
    if DEBUG >= 2:
        dbg["at"] = nc.dram_tensor("dbg_at", [B * S, 256], BF16, kind="ExternalOutput").ap()
        dbg["cbc"] = nc.dram_tensor("dbg_cbc", [128, 32], F32, kind="ExternalOutput").ap()
    if DEBUG >= 3:
        dbg["acc"] = nc.dram_tensor("dbg_acc", [128, 16 * GR], F32, kind="ExternalOutput").ap()
        dbg["h"] = nc.dram_tensor("dbg_h", [128, 32 * GR], BF16, kind="ExternalOutput").ap()

    ctx = ExitStack()
    with ctx:
        def sb(nm, shp, dt):
            return ctx.enter_context(nc.sbuf_tensor(nm, shp, dt))

        ones_col = sb("ones_col", [128, 1], BF16)
        ones_f32 = sb("ones_f32", [1, 128], F32)
        ones_rbf = sb("ones_rbf", [1, 128], BF16)
        small = sb("small", [128, 8], F32)
        mrow = sb("mrow", [1, 64], F32)
        cqt_bf = sb("cqt_bf", [1, 8], BF16)
        mbc = sb("mbc", [128, 12], F32)
        gsc_sb = sb("gsc_sb", [128, 64], F32)
        usc_sb = sb("usc_sb", [128, 64], F32)
        dsc_sb = sb("dsc_sb", [128, 16], F32)
        dscn_sb = sb("dscn_sb", [128, 16], F32)
        ident_sb = sb("ident_sb", [128, 128], F32)
        ao_sel = sb("ao_sel", [128, 16 * 16], BF16)
        bo_sel = sb("bo_sel", [16, D], BF16)
        bn6 = sb("bn6", [128, 24], F32)
        mv_sb = sb("mv_sb", [128, 4], F32)
        eps_sb = sb("eps_sb", [128, 1], F32)
        zro_sb = sb("zro_sb", [128, 1], F32)

        def s(nm):
            return Sem(ctx, nc, nm)

        sld = s("sld"); sldb = s("sldb"); sself = s("sself")
        sxl = s("sxl"); scc = s("scc")
        sst = s("sst"); smv = s("smv"); ssq = s("ssq"); sln = s("sln")
        sxst = s("sxst"); ssel = s("ssel"); sbc0 = s("sbc0"); smsel = s("smsel")
        sxnt = s("sxnt"); slowm = s("slowm"); slowe = s("slowe")
        sqkm = s("sqkm"); svm = s("svm"); sqke = s("sqke"); svve = s("svve")
        srope = s("srope"); sqkv = s("sqkv"); scsl = s("scsl")
        sqn = s("sqn"); sqne = s("sqne"); ssqr = s("ssqr"); scq = s("scq")
        snegc = s("snegc"); sbcc = s("sbcc"); scbc = s("scbc")
        ssmm = s("ssmm"); sexp = s("sexp"); smsk = s("smsk"); sav = s("sav")
        snrm = s("snrm"); satst = s("satst")
        ssmm2 = s("ssmm2"); sexp2 = s("sexp2"); smsk2 = s("smsk2")
        sav2 = s("sav2"); snrm2 = s("snrm2"); satst2 = s("satst2")
        satt = s("satt"); sxnm = s("sxnm"); sgul = s("sgul"); swdl = s("swdl")
        swol = s("swol"); sgum = s("sgum"); ssil = s("ssil"); sue = s("sue")
        shh = s("shh"); scsm = s("scsm"); scse = s("scse"); sbcs = s("sbcs")
        scsb = s("scsb"); sdnm = s("sdnm"); sdev = s("sdev")
        slom = s("slom"); sloe = s("sloe"); swom = s("swom"); swoe = s("swoe")
        strp = s("strp"); syv = s("syv"); syst = s("syst")
        scor = s("scor"); sdbg = s("sdbg")
        swcl = s("swcl"); swoc = s("swoc"); swst = s("swst")

        ps = [ctx.enter_context(nc.psum_tensor(f"ps{i}", [128, 512], F32))
              for i in range(8)]

        blk = ctx.enter_context(nc.Block())
        MM = nc.tensor.matmul

        ctx0 = ExitStack()

        def sb0(nm, shp, dt):
            return ctx0.enter_context(nc.sbuf_tensor(nm, shp, dt))

        wa_sb = sb0("wa_sb", [128, 16 * NL * 96], BF16)
        wb_sb = sb0("wb_sb", [96, NL * 256], BF16)
        wao_sb = sb0("wao_sb", [128, 16 * NL * 16], BF16)
        wbo_sb = sb0("wbo_sb", [32, NL * D], BF16)
        x0_sb = sb0("x0_sb", [128, 2 * D], F32)
        xn_sb = sb0("xn_sb", [128, 2 * D], BF16)
        wocv = sb0("wocv", [128, 2 * D], F32)
        wocb = sb0("wocb", [128, 2 * D], BF16)

        @blk.gpsimd
        def _(g):
            g.memset(ones_col[:, :], 1.0)
            g.memset(ones_f32[:, :], 1.0)
            g.memset(ones_rbf[:, :], 1.0)
            g.memset(eps_sb[:, :], 1e-5)
            g.memset(zro_sb[:, :], 0.0)
            sld.inc(g.dma_start(
                out=wa_sb[:, :].rearrange("p (c o) -> p c o", c=16),
                in_=waT.rearrange("(c p) o -> p c o", p=128)), 16)
            sld.inc(g.dma_start(out=wb_sb[:, :], in_=wbT), 16)
            sld.inc(g.dma_start(
                out=wao_sb[:, :].rearrange("p (c o) -> p c o", c=16),
                in_=waoT.rearrange("(c p) o -> p c o", p=128)), 16)
            sld.inc(g.dma_start(out=wbo_sb[:, :], in_=wboT), 16)
            sld.inc(g.dma_start(out=gsc_sb[:, :], in_=gsc_d), 16)
            sld.inc(g.dma_start(out=usc_sb[:, :], in_=usc_d), 16)
            sld.inc(g.dma_start(out=dsc_sb[:, :], in_=dsc_d), 16)
            sld.inc(g.dma_start(out=ident_sb[:, :], in_=ident_d), 16)
            sld.inc(g.dma_start(out=mrow[0:1, 0:4], in_=starts_d), 16)
            sld.inc(g.dma_start(out=mrow[0:1, 8:12], in_=iota4_d), 16)
            sld.inc(g.dma_start(out=mrow[0:1, 16:18], in_=bsel_d), 16)
            for os_ in range(16):
                if os_ >= 2:
                    g.wait_ge(swoc.h, os_ - 1)
                swcl.inc(g.dma_start(out=wocv[:, (os_ % 2) * D:(os_ % 2 + 1) * D],
                                     in_=woT_t[os_, :, :]), 16)

        @blk.scalar
        def _(sc):
            for os_ in range(16):
                sc.wait_ge(swcl.h, 16 * (os_ + 1))
                if os_ >= 2:
                    sc.wait_ge(swst.h, 16 * (os_ - 1))
                ii = nc.scalar.activation(wocb[:, (os_ % 2) * D:(os_ % 2 + 1) * D],
                                          wocv[:, (os_ % 2) * D:(os_ % 2 + 1) * D],
                                          AF.Copy)
                swoc.inc(ii, 1)
            sxl.inc(sc.dma_start(out=x0_sb[:, 0:D], in_=x_rows[0:128, :]), 16)
            sxl.inc(sc.dma_start(out=x0_sb[:, D:2 * D], in_=x_rows[128:256, :]), 16)
            sc.wait_ge(sld.h, 16)
            for t in range(8):
                sc.wait_ge(smv.h, t + 1)
                ii = nc.scalar.activation(mv_sb[:, 2:3], mv_sb[:, 1:2], AF.Sqrt,
                                          bias=eps_sb[:, 0:1], scale=1.0)
                ssq.inc(ii, 1)
                if t + 2 < 8:
                    sc.wait_ge(sln.h, t + 1)
                    sxl.inc(sc.dma_start(
                        out=x0_sb[:, ((t + 2) % 2) * D:((t + 2) % 2 + 1) * D],
                        in_=x_rows[(t + 2) * 128:(t + 3) * 128, :]), 16)

        @blk.vector
        def _(v):
            for t in range(8):
                v.wait_ge(sxl.h, 16 * (t + 1))
                if t >= 1:
                    v.wait_ge(ssq.h, t)
                xt = x0_sb[:, (t % 2) * D:(t % 2 + 1) * D]
                for j in range(4):
                    ii = nc.vector.bn_stats(bn6[:, j * 6:(j + 1) * 6],
                                            xt[:, j * 512:(j + 1) * 512])
                sst.inc(ii, 1)
                v.wait_ge(sst.h, sst.n)
                ii = nc.vector.bn_aggr(mv_sb[:, 0:2], bn6[:, :])
                smv.inc(ii, 1)
                v.wait_ge(ssq.h, t + 1)
                nc.vector.reciprocal(mv_sb[:, 3:4], mv_sb[:, 2:3])
                ii = nc.vector.tensor_scalar(
                    xn_sb[:, (t % 2) * D:(t % 2 + 1) * D], xt,
                    mv_sb[:, 0:1], mv_sb[:, 3:4], ALU.subtract, ALU.mult)
                sln.inc(ii, 1)

        @blk.sync
        def _(sy):
            for os_ in range(16):
                sy.wait_ge(swoc.h, os_ + 1)
                swst.inc(sy.dma_start(out=wo_bf[os_, :, :],
                                      in_=wocb[:, (os_ % 2) * D:(os_ % 2 + 1) * D]), 16)
            for t in range(8):
                sy.wait_ge(sln.h, t + 1)
                sxst.inc(sy.dma_start(out=xn_mine[t * 128:(t + 1) * 128, :],
                                      in_=xn_sb[:, (t % 2) * D:(t % 2 + 1) * D]), 16)

        @blk.gpsimd
        def _(g):
            g.wait_ge(sxst.h, 16 * 4)
            scc.inc(nc.gpsimd.collective_compute(
                "AllGather", ALU.bypass, replica_groups=RG,
                ins=[xn_mine[0:512, :].opt()], outs=[xn_allA.opt()]), 1)
            g.wait_ge(sxst.h, 16 * 8)
            scc.inc(nc.gpsimd.collective_compute(
                "AllGather", ALU.bypass, replica_groups=RG,
                ins=[xn_mine[512:1024, :].opt()], outs=[xn_allB.opt()]), 1)

        @blk.vector
        def _(v):
            v.wait_ge(sld.h, sld.n)
            for b in range(2):
                ii = nc.vector.tensor_scalar(mrow[0:1, 24 + 4 * b:28 + 4 * b],
                                             mrow[0:1, 0:4], float(b), None,
                                             ALU.is_le)
                smsel.inc(ii, 1)
                v.wait_ge(smsel.h, smsel.n)
                ii = nc.vector.tensor_reduce(mrow[0:1, 32 + b:33 + b],
                                             mrow[0:1, 24 + 4 * b:28 + 4 * b],
                                             AX.X, ALU.add)
                smsel.inc(ii, 1)
                v.wait_ge(smsel.h, smsel.n)
                ii = nc.vector.tensor_scalar(mrow[0:1, 32 + b:33 + b],
                                             mrow[0:1, 32 + b:33 + b],
                                             1.0, 0.0, ALU.subtract, ALU.max)
                smsel.inc(ii, 1)
                v.wait_ge(smsel.h, smsel.n)
                ii = nc.vector.tensor_scalar(mrow[0:1, 32 + b:33 + b],
                                             mrow[0:1, 32 + b:33 + b],
                                             3.0, None, ALU.min)
                smsel.inc(ii, 1)
                v.wait_ge(smsel.h, smsel.n)
                ii = nc.vector.tensor_scalar(mrow[0:1, 36 + 4 * b:40 + 4 * b],
                                             mrow[0:1, 8:12],
                                             mrow[0:1, 32 + b:33 + b], None,
                                             ALU.is_equal)
                smsel.inc(ii, 1)
            v.wait_ge(smsel.h, smsel.n)
            ii = nc.vector.tensor_scalar(mrow[0:1, 44:48], mrow[0:1, 36:40],
                                         mrow[0:1, 16:17], None, ALU.mult)
            smsel.inc(ii, 1)
            v.wait_ge(smsel.h, smsel.n)
            ii = nc.vector.scalar_tensor_tensor(
                mrow[0:1, 44:48], mrow[0:1, 40:44], mrow[0:1, 17:18],
                mrow[0:1, 44:48], ALU.mult, ALU.add)
            smsel.inc(ii, 1)

        @blk.tensor
        def _(te):
            te.wait_ge(smsel.h, smsel.n)
            MM(ps[6][:, 0:4], ones_f32[0:1, :], mrow[0:1, 36:40],
               start=True, stop=True)
            MM(ps[6][:, 4:8], ones_f32[0:1, :], mrow[0:1, 40:44],
               start=True, stop=True, skip_group_check=True)
            ii = MM(ps[6][:, 8:12], ones_f32[0:1, :], mrow[0:1, 44:48],
                    start=True, stop=True, skip_group_check=True)
            sbc0.inc(ii, 1)

        @blk.vector
        def _(v):
            v.wait_ge(sbc0.h, 1)
            ii = nc.vector.tensor_copy(mbc[:, 0:12], ps[6][:, 0:12])
            smsel.inc(ii, 1)
            v.wait_ge(smsel.h, smsel.n)
            nc.vector.tensor_scalar(dscn_sb[:, :], dsc_sb[:, :], -128.0, None,
                                    ALU.mult)
            for dc in range(16):
                dst = ao_sel[:, dc * 16:(dc + 1) * 16]
                nc.vector.tensor_scalar(
                    dst, wao_sb[:, (dc * NL + 0) * 16:(dc * NL + 1) * 16],
                    mbc[:, 8:9], None, ALU.mult)
                for a in range(1, NL):
                    nc.vector.scalar_tensor_tensor(
                        dst, wao_sb[:, (dc * NL + a) * 16:(dc * NL + a + 1) * 16],
                        mbc[:, 8 + a:9 + a], dst, ALU.mult, ALU.add)
            nc.vector.tensor_scalar(bo_sel[:, :], wbo_sb[0:16, 0:D],
                                    mbc[0:16, 8:9], None, ALU.mult)
            for a in range(1, NL):
                ii = nc.vector.scalar_tensor_tensor(
                    bo_sel[:, :], wbo_sb[0:16, a * D:(a + 1) * D],
                    mbc[0:16, 8 + a:9 + a], bo_sel[:, :], ALU.mult, ALU.add)
            smsel.inc(ii, 1)

        ctx0.close()
        ctxBC = ExitStack()

        def sbc(nm, shp, dt):
            return ctxBC.enter_context(nc.sbuf_tensor(nm, shp, dt))

        q_res = sbc("q_res", [128, HBS * S], BF16)
        k_res = sbc("k_res", [128, HBS * S], BF16)
        v_res = sbc("v_res", [128, HBS * VHB], BF16)
        wqkv_sb = sbc("wqkv_sb", [128, 16 * 768], BF16)
        a_sel = sbc("a_sel", [128, 2 * 16 * 96], BF16)
        b_sel = sbc("b_sel", [96, 2 * 256], BF16)
        cmask_sb = sbc("cmask_sb", [128, 2048], BF16)
        xnT = sbc("xnT", [128, 2 * 16 * STILE], BF16)
        low_sb = sbc("low_sb", [96, 2 * STILE], BF16)
        qtmp = sbc("qtmp", [128, 4 * STILE], BF16)
        exp_sb = sbc("exp_sb", [128, 8 * STILE], BF16)
        at_st = sbc("at_st", [128, 4 * STILE], BF16)
        cosb = sbc("cosb", [128, 2 * STILE], F32)
        sinb = sbc("sinb", [128, 2 * STILE], F32)
        cbc_sb = sbc("cbc_sb", [128, 32], F32)
        TORD = [2 * i for i in range(8)] + [2 * i + 1 for i in range(8)]

        @blk.gpsimd
        def _(g):
            g.wait_ge(smsel.h, smsel.n)
            g.wait_ge(sxst.h, 16 * 8)
            for hb in range(HBS):
                for ch in range(32):
                    g.memset(v_res[:, hb * VHB + ch * VST + 128:
                                   hb * VHB + ch * VST + 129], 1.0)
            sldb.inc(g.dma_start(
                out=wqkv_sb[:, :].rearrange("p (c o) -> p c o", c=16),
                in_=wqkvT.rearrange("(c p) o -> p c o", p=128)), 16)
            sldb.inc(g.dma_start(out=cmask_sb[:, :], in_=cmask_d), 16)
            sself.inc(g.dma_start(
                out=xnT[:, 0:NL * 96 * 16].rearrange("p (c o) -> p c o", c=16),
                in_=waT.rearrange("(c p) o -> p c o", p=128)), 16)
            sself.inc(g.dma_start(out=low_sb[0:96, 0:1024], in_=wbT), 16)
            for ti, t in enumerate(TORD):
                if ti >= 2:
                    g.wait_ge(srope.h, 4 * (ti - 1))
                colc = (t % 8) * STILE
                scsl.inc(g.dma_start(out=cosb[:, (ti % 2) * STILE:(ti % 2 + 1) * STILE],
                                     in_=cos_d[:, colc:colc + STILE]), 16)
                scsl.inc(g.dma_start(out=sinb[:, (ti % 2) * STILE:(ti % 2 + 1) * STILE],
                                     in_=sin_d[:, colc:colc + STILE]), 16)

        @blk.vector
        def _(v):
            v.wait_ge(sself.h, sself.n)
            for b in range(2):
                for dc in range(16):
                    dst = a_sel[:, (b * 16 + dc) * 96:(b * 16 + dc + 1) * 96]
                    nc.vector.tensor_scalar(
                        dst, xnT[:, (dc * NL + 0) * 96:(dc * NL + 1) * 96],
                        mbc[:, 4 * b:4 * b + 1], None, ALU.mult)
                    for a in range(1, NL):
                        nc.vector.scalar_tensor_tensor(
                            dst, xnT[:, (dc * NL + a) * 96:(dc * NL + a + 1) * 96],
                            mbc[:, 4 * b + a:4 * b + a + 1], dst, ALU.mult, ALU.add)
                dstb = b_sel[:, b * 256:(b + 1) * 256]
                nc.vector.tensor_scalar(dstb, low_sb[0:96, 0:256],
                                        mbc[0:96, 4 * b:4 * b + 1], None, ALU.mult)
                for a in range(1, NL):
                    ii = nc.vector.scalar_tensor_tensor(
                        dstb, low_sb[0:96, a * 256:(a + 1) * 256],
                        mbc[0:96, 4 * b + a:4 * b + a + 1], dstb, ALU.mult, ALU.add)
            ssel.inc(ii, 1)

        def xnt_t(ti, dc):
            return xnT[:, ((ti % 2) * 16 + dc) * STILE:((ti % 2) * 16 + dc + 1) * STILE]

        @blk.sync
        def _(sy):
            sy.wait_ge(ssel.h, 1)
            for ti, t in enumerate(TORD):
                sy.wait_ge(scc.h, 1 if ti < 8 else 2)
                if ti >= 2:
                    sy.wait_ge(sqkv.h, ti - 1)
                r, half = t // 2, t % 2
                xsrc = xn_allA if half == 0 else xn_allB
                src = xsrc[r * STILE:(r + 1) * STILE, :]
                dst = xnT[:, (ti % 2) * 16 * STILE:((ti % 2) + 1) * 16 * STILE] \
                    .rearrange("p (c s) -> p c s", c=16)
                sxnt.inc(sy.dma_start(out=dst, in_=src, transpose=True), 16)

        @blk.tensor
        def _(te):
            te.wait_ge(sldb.h, sldb.n)
            te.wait_ge(ssel.h, 1)
            for ti, t in enumerate(TORD):
                b = t // 8
                te.wait_ge(sxnt.h, 16 * (ti + 1))
                if ti >= 1:
                    te.wait_ge(slowe.h, ti)
                for dc in range(16):
                    ii = MM(ps[6][0:96, 0:STILE],
                            a_sel[:, (b * 16 + dc) * 96:(b * 16 + dc + 1) * 96],
                            xnt_t(ti, dc), start=(dc == 0), stop=(dc == 15),
                            skip_group_check=True)
                slowm.inc(ii, 1)
                if ti >= 1:
                    te.wait_ge(sqke.h, 4 * ti)
                    te.wait_ge(svve.h, ti)
                for pi in range(4):
                    wcol = pi * 128 if pi < 2 else 256 + (pi - 2) * 128
                    for dc in range(16):
                        MM(ps[pi][:, 0:STILE],
                           wqkv_sb[:, dc * 768 + wcol: dc * 768 + wcol + 128],
                           xnt_t(ti, dc), start=(dc == 0), stop=False,
                           skip_group_check=True)
                for ss in range(4):
                    bank = 4 if ss < 2 else 5
                    off = (ss % 2) * 256
                    for dc in range(16):
                        MM(ps[bank][:, off:off + 256],
                           xnt_t(ti, dc)[:, ss * 128:(ss + 1) * 128],
                           wqkv_sb[:, dc * 768 + 512: dc * 768 + 768],
                           start=(dc == 0 and ss % 2 == 0), stop=False,
                           skip_group_check=True)
                te.wait_ge(slowe.h, ti + 1)
                for pi in range(4):
                    proj, hi = pi // 2, pi % 2
                    ii = MM(ps[pi][:, 0:STILE],
                            b_sel[proj * 32: proj * 32 + 16,
                                  b * 256 + hi * 128: b * 256 + (hi + 1) * 128],
                            low_sb[proj * 32: proj * 32 + 16,
                                   (ti % 2) * STILE:(ti % 2 + 1) * STILE],
                            start=False, stop=True, skip_group_check=True)
                    sqkm.inc(ii, 1)
                for ss in range(4):
                    bank = 4 if ss < 2 else 5
                    off = (ss % 2) * 256
                    ii = MM(ps[bank][:, off:off + 256],
                            low_sb[64:80, (ti % 2) * STILE + ss * 128:
                                   (ti % 2) * STILE + (ss + 1) * 128],
                            b_sel[64:80, b * 256:(b + 1) * 256],
                            start=False, stop=(ss % 2 == 1), skip_group_check=True)
                    if ss % 2 == 1:
                        svm.inc(ii, 1)
                sqkv.inc(te.nop(), 1)

        @blk.scalar
        def _(sc):
            for ti in range(NT):
                for pi in range(4):
                    sc.wait_ge(sqkm.h, 4 * ti + pi + 1)
                    if ti >= 1:
                        sc.wait_ge(srope.h, 4 * (ti - 1) + pi + 1)
                    ii = nc.scalar.activation(qtmp[:, pi * STILE:(pi + 1) * STILE],
                                              ps[pi][:, 0:STILE], AF.Copy)
                    sqke.inc(ii, 1)

        @blk.vector
        def _(v):
            for ti, t in enumerate(TORD):
                b = t // 8
                v.wait_ge(slowm.h, ti + 1)
                ii = nc.vector.tensor_copy(
                    low_sb[:, (ti % 2) * STILE:(ti % 2 + 1) * STILE],
                    ps[6][0:96, 0:STILE])
                slowe.inc(ii, 1)
                for ss in range(4):
                    v.wait_ge(svm.h, 2 * ti + 1 + (1 if ss >= 2 else 0))
                    bank = 4 if ss < 2 else 5
                    off = (ss % 2) * 256
                    ch = (t % 8) * 4 + ss
                    for hi in range(2):
                        hb = b * 2 + hi
                        ii = nc.vector.tensor_copy(
                            v_res[:, hb * VHB + ch * VST: hb * VHB + ch * VST + 128],
                            ps[bank][:, off + hi * 128: off + (hi + 1) * 128])
                svve.inc(ii, 1)
                for pi in range(4):
                    v.wait_ge(sqke.h, 4 * ti + pi + 1)
                    v.wait_ge(scsl.h, 32 * (ti + 1))
                    isq, hi = pi < 2, pi % 2
                    hb = b * 2 + hi
                    res = q_res if isq else k_res
                    slot = qtmp[:, pi * STILE:(pi + 1) * STILE]
                    colc = (t % 8) * STILE
                    cs_ = cosb[:, (ti % 2) * STILE:(ti % 2 + 1) * STILE]
                    sn_ = sinb[:, (ti % 2) * STILE:(ti % 2 + 1) * STILE]
                    dst = res[:, hb * S + colc: hb * S + colc + STILE]
                    x1, x2 = slot[0:64, :], slot[64:128, :]
                    w1 = at_st[0:64, 0:STILE]
                    w2 = at_st[64:128, 0:STILE]
                    nc.vector.tensor_tensor(w1, x2, sn_[64:128, :], ALU.mult)
                    nc.vector.tensor_tensor(w2, x1, sn_[0:64, :], ALU.mult)
                    nc.vector.tensor_tensor(dst[0:64, :], x1, cs_[0:64, :], ALU.mult)
                    nc.vector.tensor_tensor(dst[64:128, :], x2, cs_[64:128, :], ALU.mult)
                    nc.vector.tensor_tensor(dst[0:64, :], dst[0:64, :], w1,
                                            ALU.subtract)
                    ii = nc.vector.tensor_tensor(dst[64:128, :], dst[64:128, :], w2,
                                                 ALU.add)
                    srope.inc(ii, 1)

        LKC = [(qt, kc) for qt in range(QT) for kc in _kcs(qt)]
        L = len(LKC)
        QEND = {}
        for li, (qt, kc) in enumerate(LKC):
            QEND[qt] = li
        DRANK = {}
        _dcnt = [0, 0]
        for p_ in range(2):
            for li_ in range(L):
                for ss_ in range(2):
                    qt_, kc_ = LKC[li_]
                    if kc_ >= 4 * qt_:
                        DRANK[(p_, ss_, li_)] = _dcnt[ss_]
                        _dcnt[ss_] += 1
        SSMM = [ssmm, ssmm2]
        SEXP = [sexp, sexp2]
        SMSK = [smsk, smsk2]
        SAV = [sav, sav2]
        SNRM = [snrm, snrm2]
        SATST = [satst, satst2]


        @blk.scalar
        def _(sc):
            for hb in range(HBS):
                sc.wait_ge(scq.h, 2 * hb + 2)
                ii = nc.scalar.activation(mrow[0:1, 56:64], mrow[0:1, 32:40],
                                          AF.Sqrt, bias=zro_sb[0:1, 0:1],
                                          scale=1.0 / 128.0)
                ssqr.inc(ii, 1)

        @blk.tensor
        def _(te):
            te.wait_ge(svve.h, NT)
            te.wait_ge(slowe.h, NT)
            te.wait_ge(sqke.h, 4 * NT)
            te.wait_ge(srope.h, 4 * NT)
            for hb in range(HBS):
                for j in range(16):
                    te.wait_ge(ssqr.h, 3 * hb + (1 if j < 8 else 2))
                    if j == 0 and hb >= 1:
                        te.wait_ge(scbc.h, hb)
                    if j >= 2:
                        te.wait_ge(sqne.h, 16 * hb + j - 1)
                    ii = MM(ps[6 + j % 2][0:1, 0:STILE], ones_col[:, 0:1],
                            xnT[:, j * 512:(j + 1) * 512],
                            start=True, stop=True, skip_group_check=True)
                    sqn.inc(ii, 1)
                te.wait_ge(snegc.h, 2 * hb + 2)
                ii = MM(ps[6][:, 0:8], ones_rbf[0:1, :], cqt_bf[0:1, 0:8],
                        start=True, stop=True, skip_group_check=True)
                sbcc.inc(ii, 1)

        @blk.vector
        def _(v):
            for hb in range(HBS):
                if hb >= 1:
                    v.wait_ge(sqn.h, 16 * hb)
                ii = nc.vector.tensor_tensor(xnT[:, 0:S],
                                             q_res[:, hb * S:(hb + 1) * S],
                                             q_res[:, hb * S:(hb + 1) * S], ALU.mult)
                ssqr.inc(ii, 1)
                ii = nc.vector.tensor_tensor(xnT[:, S:2 * S],
                                             k_res[:, hb * S:(hb + 1) * S],
                                             k_res[:, hb * S:(hb + 1) * S], ALU.mult)
                ssqr.inc(ii, 1)
                for j in range(16):
                    v.wait_ge(sqn.h, 16 * hb + j + 1)
                    dst = mrow[0:1, 24 + j:25 + j] if j < 8 else \
                        mrow[0:1, 40 + (j - 8):41 + (j - 8)]
                    ii = nc.vector.tensor_reduce(dst, ps[6 + j % 2][0:1, 0:STILE],
                                                 AX.X, ALU.max)
                    sqne.inc(ii, 1)
                v.wait_ge(sqne.h, sqne.n)
                ii = nc.vector.tensor_reduce(mrow[0:1, 48:49], mrow[0:1, 40:48],
                                             AX.X, ALU.max)
                scq.inc(ii, 1)
                v.wait_ge(scq.h, scq.n)
                ii = nc.vector.tensor_scalar(mrow[0:1, 32:40], mrow[0:1, 24:32],
                                             mrow[0:1, 48:49], None, ALU.mult)
                scq.inc(ii, 1)
                v.wait_ge(ssqr.h, 3 * hb + 3)
                ii = nc.vector.tensor_scalar(cqt_bf[0:1, 0:8], mrow[0:1, 56:64],
                                             -1.0, None, ALU.mult)
                snegc.inc(ii, 2)
                v.wait_ge(sbcc.h, hb + 1)
                ii = nc.vector.tensor_copy(cbc_sb[:, hb * 8:(hb + 1) * 8],
                                           ps[6][:, 0:8])
                scbc.inc(ii, 1)

        @blk.tensor
        def _(te):
            te.wait_ge(scbc.h, 4)

            def emit_av(p, ss, lj):
                hb = 2 * p + ss
                qt2, kc2 = LKC[lj]
                glj = p * L + lj
                abanks = (2, 3) if ss == 0 else (4, 5)
                if kc2 >= 4 * qt2:
                    te.wait_ge(SMSK[ss].h, DRANK[(p, ss, lj)] + 1)
                else:
                    te.wait_ge(SEXP[ss].h, glj + 1)
                if kc2 == 0:
                    te.wait_ge(SNRM[ss].h, 8 * p + qt2)
                slot = exp_sb[:, (ss * 4 + lj % 4) * STILE:
                              (ss * 4 + lj % 4 + 1) * STILE]
                for qs in range(4):
                    bank = abanks[qs // 2]
                    off = (qs % 2) * 256
                    ii = MM(ps[bank][:, off:off + 129],
                            slot[:, qs * 128:(qs + 1) * 128],
                            v_res[:, hb * VHB + kc2 * VST: hb * VHB + kc2 * VST + 129],
                            start=(kc2 == 0 and qs % 2 == 0), stop=False,
                            skip_group_check=True)
                SAV[ss].inc(ii, 1)

            for p in range(2):
                for li in range(L):
                    for ss in range(2):
                        hb = 2 * p + ss
                        qt, kc = LKC[li]
                        gli = p * L + li
                        sbanks = (0, 1) if ss == 0 else (6, 7)
                        if gli >= 2:
                            te.wait_ge(SEXP[ss].h, gli - 1)
                        ii = MM(ps[sbanks[li % 2]][:, 0:STILE],
                                k_res[:, hb * S + kc * 128: hb * S + (kc + 1) * 128],
                                q_res[:, hb * S + qt * STILE: hb * S + (qt + 1) * STILE],
                                start=True, stop=True, skip_group_check=True)
                        SSMM[ss].inc(ii, 1)
                        if li >= 1:
                            emit_av(p, ss, li - 1)
                emit_av(p, 0, L - 1)
                emit_av(p, 1, L - 1)

        @blk.scalar
        def _(sc):
            for p in range(2):
                sc.wait_ge(scbc.h, 2 * p + 2)
                for li in range(L):
                    for ss in range(2):
                        hb = 2 * p + ss
                        qt, kc = LKC[li]
                        gli = p * L + li
                        sc.wait_ge(SSMM[ss].h, gli + 1)
                        if gli - 4 >= 0:
                            sc.wait_ge(SAV[ss].h, gli - 3)
                        ii = nc.scalar.activation(
                            exp_sb[:, (ss * 4 + li % 4) * STILE:
                                   (ss * 4 + li % 4 + 1) * STILE],
                            ps[((0, 1) if ss == 0 else (6, 7))[li % 2]][:, 0:STILE],
                            AF.Exp,
                            bias=cbc_sb[:, hb * 8 + qt: hb * 8 + qt + 1],
                            scale=INV_SQRT_DK)
                        SEXP[ss].inc(ii, 1)

        @blk.vector
        def _(v):
            for p in range(2):
                for li in range(L):
                    for ss in range(2):
                        qt, kc = LKC[li]
                        gli = p * L + li
                        if kc >= 4 * qt:
                            v.wait_ge(SEXP[ss].h, gli + 1)
                            voff = (kc - 4 * qt) * STILE
                            slot = exp_sb[:, (ss * 4 + li % 4) * STILE:
                                          (ss * 4 + li % 4 + 1) * STILE]
                            ii = nc.vector.tensor_tensor(
                                slot, slot, cmask_sb[:, voff:voff + STILE], ALU.mult)
                            SMSK[ss].inc(ii, 1)
                        if li == QEND[qt]:
                            gqt = 8 * p + qt
                            v.wait_ge(SAV[ss].h, p * L + li + 1)
                            if gqt >= 2:
                                v.wait_ge(SATST[ss].h, 16 * (gqt - 1))
                            abanks = (2, 3) if ss == 0 else (4, 5)
                            stg = at_st[:, (ss * 2 + qt % 2) * STILE:
                                        (ss * 2 + qt % 2 + 1) * STILE]
                            for qs in range(4):
                                bank = abanks[qs // 2]
                                off = (qs % 2) * 256
                                ii = nc.vector.reciprocal(
                                    small[:, qs:qs + 1],
                                    ps[bank][:, off + 128:off + 129])
                            smsel.inc(ii, 1)
                            v.wait_ge(smsel.h, smsel.n)
                            for qs in range(4):
                                bank = abanks[qs // 2]
                                off = (qs % 2) * 256
                                ii = nc.vector.tensor_scalar(
                                    stg[:, qs * 128:(qs + 1) * 128],
                                    ps[bank][:, off:off + 128],
                                    small[:, qs:qs + 1], None, ALU.mult)
                            SNRM[ss].inc(ii, 1)

        @blk.sync
        def _(sy):
            for p in range(2):
                for qt in range(QT):
                    for ss in range(2):
                        hb = 2 * p + ss
                        b, hi = hb // 2, hb % 2
                        gqt = 8 * p + qt
                        sy.wait_ge(SNRM[ss].h, gqt + 1)
                        stg = at_st[:, (ss * 2 + qt % 2) * STILE:
                                    (ss * 2 + qt % 2 + 1) * STILE]
                        dst = at_mine[b * S + qt * STILE: b * S + (qt + 1) * STILE,
                                      hi * 128:(hi + 1) * 128]
                        SATST[ss].inc(sy.dma_start(
                            out=dst.rearrange("(a p) d -> p a d", p=128),
                            in_=stg.rearrange("p (a d) -> p a d", a=4)), 16)

        @blk.gpsimd
        def _(g):
            g.wait_ge(satst.h, 16 * 8)
            g.wait_ge(satst2.h, 16 * 8)
            scc.inc(nc.gpsimd.collective_compute(
                "AllGather", ALU.bypass, replica_groups=RG,
                ins=[at_mine[0:S, :].opt()], outs=[at_allA.opt()]), 1)
            g.wait_ge(satst.h, 16 * 16)
            g.wait_ge(satst2.h, 16 * 16)
            scc.inc(nc.gpsimd.collective_compute(
                "AllGather", ALU.bypass, replica_groups=RG,
                ins=[at_mine[S:B * S, :].opt()], outs=[at_allB.opt()]), 1)

        if DEBUG >= 1:
            @blk.gpsimd
            def _(g):
                g.wait_ge(scc.h, 2)
                sdbg.inc(g.dma_start(out=dbg["xn"][0:NC * 512, :], in_=xn_allA), 16)
                sdbg.inc(g.dma_start(out=dbg["xn"][NC * 512:, :], in_=xn_allB), 16)
                g.wait_ge(sqkv.h, NT)
                g.wait_ge(srope.h, 4 * NT)
                g.wait_ge(svve.h, NT)
                sdbg.inc(g.dma_start(out=dbg["q"], in_=q_res[:, :]), 16)
                sdbg.inc(g.dma_start(out=dbg["k"], in_=k_res[:, :]), 16)
                sdbg.inc(g.dma_start(out=dbg["v"], in_=v_res[:, :]), 16)
                g.wait_ge(slowe.h, NT)
                sdbg.inc(g.dma_start(out=dbg["low"], in_=low_sb[:, :]), 16)
        if DEBUG >= 2:
            @blk.gpsimd
            def _(g):
                g.wait_ge(scc.h, 5)
                sdbg.inc(g.dma_start(out=dbg["at"], in_=at_mine), 16)
                g.wait_ge(scbc.h, 4)
                sdbg.inc(g.dma_start(out=dbg["cbc"], in_=cbc_sb[:, :]), 16)

        ctxBC.close()

        attnT = sb("attnT", [128, 16 * GR], BF16)
        acc = sb("acc", [128, 16 * GR], F32)
        xnTm = sb("xnTm", [128, 16 * GR], BF16)
        wg_sb = sb("wg_sb", [128, 2 * 4096], BF16)
        wu_sb = sb("wu_sb", [128, 2 * 4096], BF16)
        wd_sb = sb("wd_sb", [128, 3 * 4096], BF16)
        wo_sb = sb("wo_sb", [128, 2 * D], BF16)
        h_sb = sb("h_sb", [128, 32 * GR], BF16)
        sgu_t = sb("sgu_t", [128, 2 * GR], BF16)
        ut_t = sb("ut_t", [128, 2 * GR], BF16)
        low_o = sb("low_o", [16, GR], BF16)
        cs_bc = sb("cs_bc", [128, GR], F32)
        csn_r = sb("csn_r", [1, GR], F32)
        y_st = sb("y_st", [128, D], F32)
        xE_sb = sb("xE_sb", [128, 2 * D], F32)

        @blk.gpsimd
        def _(g):
            g.wait_ge(satst.h, 16 * 12)
            g.wait_ge(satst2.h, 16 * 12)
            scc.inc(nc.gpsimd.collective_compute(
                "AllGather", ALU.bypass, replica_groups=RG,
                ins=[at_mine[S:S + 2048, :].opt()],
                outs=[at_allB[0:NC * 2048, :].opt()]), 1)
            g.wait_ge(satst.h, 16 * 16)
            g.wait_ge(satst2.h, 16 * 16)
            scc.inc(nc.gpsimd.collective_compute(
                "AllGather", ALU.bypass, replica_groups=RG,
                ins=[at_mine[S + 2048:B * S, :].opt()],
                outs=[at_allB[NC * 2048:, :].opt()]), 1)
            for grp in range(GRP):
                for ihalf in range(IH):
                    for sl in range(NSLAB):
                        k = (grp * IH + ihalf) * NSLAB + sl
                        if k >= 2:
                            g.wait_ge(sgum.h, 4 * (k - 1))
                        sgul.inc(g.dma_start(
                            out=wg_sb[:, (k % 2) * 4096:(k % 2 + 1) * 4096],
                            in_=g_u8[ihalf * NSLAB + sl, :, :]), 16)
                        sgul.inc(g.dma_start(
                            out=wu_sb[:, (k % 2) * 4096:(k % 2 + 1) * 4096],
                            in_=u_u8[ihalf * NSLAB + sl, :, :]), 16)
                    for ms in range(16):
                        w = (grp * IH + ihalf) * 16 + ms
                        if w >= 3:
                            g.wait_ge(sdnm.h, w - 2)
                        swdl.inc(g.dma_start(
                            out=wd_sb[:, (w % 3) * 4096:(w % 3 + 1) * 4096],
                            in_=d_u8[ms, :, ihalf * 4096:(ihalf + 1) * 4096]), 16)
                        if ihalf == 1:
                            k2 = grp * 16 + ms
                            if k2 >= 2:
                                g.wait_ge(swom.h, k2 - 1)
                            swol.inc(g.dma_start(
                                out=wo_sb[:, (k2 % 2) * D:(k2 % 2 + 1) * D],
                                in_=wo_bf[ms, :, :]), 16)

        @blk.sync
        def _(sy):
            sy.wait_ge(satst.h, 16 * 16)
            sy.wait_ge(satst2.h, 16 * 16)
            for grp in range(GRP):
                if grp >= 1:
                    sy.wait_ge(sgum.h, 4 * NSLAB * IH * grp)
                    sy.wait_ge(swom.h, 16 * grp)
                src = xn_mine[grp * GR:(grp + 1) * GR, :]
                sxnm.inc(sy.dma_start(
                    out=xnTm[:, :].rearrange("p (c s) -> p c s", c=16),
                    in_=src, transpose=True), 16)
                for arm in nc.sync.switch_core_id(n=128):
                    rank = arm.logical % 8
                    b, sq = rank // 4, (rank % 4) * ROWS
                    if b == 0:
                        need = 3
                    else:
                        need = 4 if (sq + (grp + 1) * GR) <= 2048 else 5
                    nc.sync.wait_ge(scc.h, need)
                    for r in range(8):
                        if b == 0:
                            src2 = at_allA[r * S + sq + grp * GR:
                                           r * S + sq + (grp + 1) * GR, :]
                        else:
                            base = sq + grp * GR
                            if base < 2048:
                                src2 = at_allB[r * 2048 + base:
                                               r * 2048 + base + GR, :]
                            else:
                                src2 = at_allB[NC * 2048 + r * 2048 + (base - 2048):
                                               NC * 2048 + r * 2048 + (base - 2048) + GR, :]
                        dst2 = attnT[:, (2 * r) * GR:(2 * r + 2) * GR] \
                            .rearrange("p (hl s) -> p hl s", hl=2)
                        sy.dma_start(out=dst2, in_=src2,
                                     transpose=True).then_inc(satt.h, 16)
                satt.n += 8 * 16
                for ss in range(4):
                    gss = grp * 4 + ss
                    sy.wait_ge(syv.h, 16 * (gss + 1))
                    syst.inc(sy.dma_start(
                        out=y_out[grp * GR + ss * 128: grp * GR + (ss + 1) * 128, :],
                        in_=y_st[:, :]), 16)
            sy.wait_ge(syst.h, syst.n)

        @blk.scalar
        def _(sc):
            for grp in range(GRP):
                for ihalf in range(IH):
                    for isl in range(32):
                        gi = (grp * IH + ihalf) * 32 + isl
                        sc.wait_ge(sgum.h, 2 * gi + 1)
                        if gi >= 2:
                            sc.wait_ge(shh.h, gi - 1)
                        ii = nc.scalar.activation(
                            sgu_t[:, (isl % 2) * GR:(isl % 2 + 1) * GR],
                            ps[2 * (isl % 2)][:, 0:GR], AF.Silu,
                            scale=gsc_sb[:, ihalf * 32 + isl:ihalf * 32 + isl + 1])
                        ssil.inc(ii, 1)
                for ss in range(4):
                    gss = grp * 4 + ss
                    if gss >= 2:
                        sc.wait_ge(syv.h, 16 * (gss - 1))
                    sxl.inc(sc.dma_start(
                        out=xE_sb[:, (gss % 2) * D:(gss % 2 + 1) * D],
                        in_=x_rows[grp * GR + ss * 128: grp * GR + (ss + 1) * 128, :]), 16)

        @blk.tensor
        def _(te):
            te.wait_ge(sav.h, 2 * L)
            te.wait_ge(sav2.h, 2 * L)
            te.wait_ge(snrm.h, 16)
            te.wait_ge(snrm2.h, 16)
            te.wait_ge(sqne.h, 64)
            te.wait_ge(scbc.h, 4)
            for grp in range(GRP):
                te.wait_ge(sxnm.h, 16 * (grp + 1))
                te.wait_ge(syv.h, 64 * grp)
                te.wait_ge(ssil.h, 64 * grp)
                te.wait_ge(sue.h, 64 * grp)
                te.wait_ge(swoe.h, 16 * grp)
                te.wait_ge(sdev.h, 32 * grp)
                te.wait_ge(satt.h, 8 * 16 * (grp + 1))
                if grp >= 1:
                    te.wait_ge(sloe.h, grp)
                for dc in range(16):
                    ii = MM(ps[7][0:16, 0:GR], ao_sel[:, dc * 16:(dc + 1) * 16],
                            attnT[:, dc * GR:(dc + 1) * GR],
                            start=(dc == 0), stop=(dc == 15), skip_group_check=True)
                slom.inc(ii, 1)
                for ihalf in range(IH):
                    for isl in range(32):
                        sl, io = isl // 2, isl % 2
                        k = (grp * IH + ihalf) * NSLAB + sl
                        gi = (grp * IH + ihalf) * 32 + isl
                        te.wait_ge(sgul.h, 32 * (k + 1))
                        gb = 2 * (isl % 2)
                        ub = gb + 1
                        if isl >= 2 or (grp * IH + ihalf) >= 1:
                            te.wait_ge(ssil.h, gi - 1)
                            te.wait_ge(sue.h, gi - 1)
                        for dc in range(16):
                            MM(ps[gb][:, 0:GR],
                               wg_sb[:, (k % 2) * 4096 + dc * 256 + io * 128:
                                     (k % 2) * 4096 + dc * 256 + io * 128 + 128],
                               xnTm[:, dc * GR:(dc + 1) * GR],
                               start=(dc == 0), stop=(dc == 15),
                               skip_group_check=True)
                        sgum.inc(MM(ps[ub][:, 0:GR],
                                    wu_sb[:, (k % 2) * 4096 + io * 128:
                                          (k % 2) * 4096 + io * 128 + 128],
                                    xnTm[:, 0:GR], start=True, stop=False,
                                    skip_group_check=True), 1)
                        for dc in range(1, 16):
                            ii = MM(ps[ub][:, 0:GR],
                                    wu_sb[:, (k % 2) * 4096 + dc * 256 + io * 128:
                                          (k % 2) * 4096 + dc * 256 + io * 128 + 128],
                                    xnTm[:, dc * GR:(dc + 1) * GR],
                                    start=False, stop=(dc == 15),
                                    skip_group_check=True)
                        sgum.inc(ii, 1)
                    te.wait_ge(shh.h, (grp * IH + ihalf) * 32 + 32)
                    if ihalf == 0 and grp >= 1:
                        te.wait_ge(scsb.h, grp)
                    for icl in range(32):
                        ii = MM(ps[4][0:1, 0:GR], ones_col[:, 0:1],
                                h_sb[:, icl * GR:(icl + 1) * GR],
                                start=(ihalf == 0 and icl == 0),
                                stop=(ihalf == 1 and icl == 31),
                                skip_group_check=True)
                    scsm.inc(ii, 1)
                    for ms in range(16):
                        w = (grp * IH + ihalf) * 16 + ms
                        te.wait_ge(swdl.h, 16 * (w + 1))
                        db = 5 + ms % 2
                        if w >= 2:
                            te.wait_ge(sdev.h, w - 1)
                        for icl in range(32):
                            ii = MM(ps[db][:, 0:GR],
                                    wd_sb[:, (w % 3) * 4096 + icl * 128:
                                          (w % 3) * 4096 + icl * 128 + 128],
                                    h_sb[:, icl * GR:(icl + 1) * GR],
                                    start=(icl == 0), stop=(icl == 31),
                                    skip_group_check=True)
                        sdnm.inc(ii, 1)
                        if ihalf == 1:
                            k2 = grp * 16 + ms
                            te.wait_ge(swol.h, 16 * (k2 + 1))
                            wb_ = ms % 2
                            if ms == 0:
                                te.wait_ge(ssil.h, 64 * (grp + 1))
                                te.wait_ge(sue.h, 64 * (grp + 1))
                                te.wait_ge(sloe.h, grp + 1)
                            if ms >= 2:
                                te.wait_ge(swoe.h, grp * 16 + ms - 1)
                            for dc in range(16):
                                MM(ps[wb_][:, 0:GR],
                                   wo_sb[:, (k2 % 2) * D + dc * 128:
                                         (k2 % 2) * D + (dc + 1) * 128],
                                   attnT[:, dc * GR:(dc + 1) * GR],
                                   start=(dc == 0), stop=False,
                                   skip_group_check=True)
                            ii = MM(ps[wb_][:, 0:GR],
                                    bo_sel[:, ms * 128:(ms + 1) * 128],
                                    low_o[:, :], start=False, stop=True,
                                    skip_group_check=True)
                            swom.inc(ii, 1)
                te.wait_ge(scse.h, grp + 1)
                ii = MM(ps[4][:, 0:GR], ones_f32[0:1, :], csn_r[0:1, :],
                        start=True, stop=True, skip_group_check=True)
                sbcs.inc(ii, 1)
                for ss in range(4):
                    for ms in range(16):
                        ti = (grp * 4 + ss) * 16 + ms
                        tb = 2 + ti % 4
                        te.wait_ge(swoe.h, 16 * (grp + 1))
                        te.wait_ge(scor.h, 16 * (grp + 1))
                        if ti >= 4:
                            te.wait_ge(syv.h, ti - 3)
                        ii = nc.tensor.transpose(
                            ps[tb][:, 0:128],
                            acc[:, ms * GR + ss * 128: ms * GR + ss * 128 + 128],
                            ident_sb[:, :])
                        strp.inc(ii, 1)

        @blk.vector
        def _(v):
            for grp in range(GRP):
                v.wait_ge(slom.h, grp + 1)
                ii = nc.vector.tensor_copy(low_o[:, :], ps[7][0:16, 0:GR])
                sloe.inc(ii, 1)
                for ihalf in range(IH):
                    for isl in range(32):
                        gi = (grp * IH + ihalf) * 32 + isl
                        v.wait_ge(sgum.h, 2 * gi + 2)
                        if gi >= 2:
                            v.wait_ge(shh.h, gi - 1)
                        ub = 2 * (isl % 2) + 1
                        ii = nc.vector.tensor_scalar(
                            ut_t[:, (isl % 2) * GR:(isl % 2 + 1) * GR],
                            ps[ub][:, 0:GR],
                            usc_sb[:, ihalf * 32 + isl:ihalf * 32 + isl + 1],
                            None, ALU.mult)
                        sue.inc(ii, 1)
                        v.wait_ge(ssil.h, gi + 1)
                        ii = nc.vector.tensor_tensor(
                            h_sb[:, isl * GR:(isl + 1) * GR],
                            sgu_t[:, (isl % 2) * GR:(isl % 2 + 1) * GR],
                            ut_t[:, (isl % 2) * GR:(isl % 2 + 1) * GR], ALU.mult)
                        shh.inc(ii, 1)
                    for ms in range(16):
                        w = (grp * IH + ihalf) * 16 + ms
                        v.wait_ge(sdnm.h, w + 1)
                        aslice = acc[:, ms * GR:(ms + 1) * GR]
                        db = 5 + ms % 2
                        if ihalf == 0:
                            ii = nc.vector.tensor_scalar(
                                aslice, ps[db][:, 0:GR],
                                dsc_sb[:, ms:ms + 1], None, ALU.mult)
                        else:
                            ii = nc.vector.scalar_tensor_tensor(
                                aslice, ps[db][:, 0:GR], dsc_sb[:, ms:ms + 1],
                                aslice, ALU.mult, ALU.add)
                        sdev.inc(ii, 1)
                        if ihalf == 1:
                            v.wait_ge(swom.h, grp * 16 + ms + 1)
                            wb_ = ms % 2
                            ii = nc.vector.tensor_tensor(
                                aslice, aslice, ps[wb_][:, 0:GR], ALU.add)
                            swoe.inc(ii, 1)
                v.wait_ge(scsm.h, 2 * (grp + 1))
                ii = nc.vector.tensor_copy(csn_r[0:1, :], ps[4][0:1, 0:GR])
                scse.inc(ii, 1)
                v.wait_ge(sbcs.h, grp + 1)
                ii = nc.vector.tensor_copy(cs_bc[:, :], ps[4][:, 0:GR])
                scsb.inc(ii, 1)
                for ms in range(16):
                    v.wait_ge(swoe.h, grp * 16 + ms + 1)
                    aslice = acc[:, ms * GR:(ms + 1) * GR]
                    ii = nc.vector.scalar_tensor_tensor(
                        aslice, cs_bc[:, :], dscn_sb[:, ms:ms + 1], aslice,
                        ALU.mult, ALU.add)
                    scor.inc(ii, 1)
                for ss in range(4):
                    gss = grp * 4 + ss
                    v.wait_ge(sxl.h, 16 * (8 + gss + 1))
                    if gss >= 1:
                        v.wait_ge(syst.h, 16 * gss)
                    for ms in range(16):
                        ti = gss * 16 + ms
                        v.wait_ge(strp.h, ti + 1)
                        tb = 2 + ti % 4
                        ii = nc.vector.tensor_tensor(
                            y_st[:, ms * 128:(ms + 1) * 128],
                            ps[tb][:, 0:128],
                            xE_sb[:, (gss % 2) * D + ms * 128:
                                  (gss % 2) * D + (ms + 1) * 128], ALU.add)
                        syv.inc(ii, 1)

        if DEBUG >= 3:
            @blk.gpsimd
            def _(g):
                g.wait_ge(syst.h, 16 * 4 * GRP)
                sdbg.inc(g.dma_start(out=dbg["h"], in_=h_sb[:, :]), 16)
                sdbg.inc(g.dma_start(out=dbg["acc"], in_=acc[:, :]), 16)

        @blk.gpsimd
        def _(g):
            g.wait_ge(syst.h, 16 * 4 * GRP)
            if DEBUG >= 1:
                g.wait_ge(sdbg.h, sdbg.n)

    return nc


def kernel(**inputs):
    in_maps = _stage(inputs)
    nc = _build()
    res = run_bass_kernel_spmd(nc, in_maps, core_ids=list(range(NC)))
    y = np.concatenate([np.asarray(res.results[c]["y"], np.float32)
                        for c in range(NC)], axis=0)
    return y.reshape(B, S, D)
